# version 06
# baseline (speedup 1.0000x reference)
import sys
for _p in ("/opt/trn_rl_repo",):
    if _p not in sys.path:
        sys.path.insert(0, _p)
import math
import numpy as np
import ml_dtypes
import concourse.bass as bass
import concourse.mybir as mybir
from concourse.bass_utils import run_bass_kernel_spmd

F32 = mybir.dt.float32
BF16 = mybir.dt.bfloat16
AF = mybir.ActivationFunctionType
ALU = mybir.AluOpType
AX = mybir.AxisListType

BF = ml_dtypes.bfloat16
TT = 2048
TH = 1024
DM = 1024
NCH = DM // 128
HID = 2730
HIDP = 2816
NEG = np.float32(-1e30)

ENGS = ("pe", "act", "dve", "pool", "sp")



QHEADS = [0, 4, 1, 5, 2, 6, 3, 7, 8, 12, 9, 13, 10, 14, 11, 15]


def _host_prep(I):
    H = {}
    f32 = lambda a: np.ascontiguousarray(np.asarray(a), dtype=np.float32)

    qperm = np.concatenate([64 * h + np.arange(64) for h in QHEADS])
    H["l0_wq"] = f32(np.asarray(I["L0_attn_Wq"])[:, qperm])
    H["l0_wk"] = f32(I["L0_attn_Wk"])
    H["l0_wv"] = f32(I["L0_attn_Wv"])
    H["l0_wo"] = f32(np.asarray(I["L0_attn_Wo"])[qperm, :])
    H["f_wqkv"] = f32(I["L0_ffn_wqkv"])
    H["f_wo"] = f32(I["L0_ffn_wo"])
    w1 = np.zeros((2, DM, 2 * HIDP), np.float32)
    w1[:, :, :HID] = np.asarray(I["L0_ffn_w1"])[:, :, :HID]
    w1[:, :, HIDP:HIDP + HID] = np.asarray(I["L0_ffn_w1"])[:, :, HID:]
    H["f_w1"] = w1
    w2 = np.zeros((2, HIDP, DM), np.float32)
    w2[:, :HID, :] = np.asarray(I["L0_ffn_w2"])
    H["f_w2"] = w2
    H["l1_wqk"] = f32(I["L1_attn_Wqk"])
    wvn = np.zeros((DM, 1056), np.float32)
    wvn[:, :1024] = np.asarray(I["L1_attn_Wv"])
    wvn[:, 1024:1040] = np.asarray(I["L1_attn_Wn"])
    wvn[:, 1040:1048] = np.asarray(I["L1_ffn_router_w"])
    H["l1_wvn"] = wvn
    H["l1_wo"] = f32(I["L1_attn_Wo"])
    H["g_w1"] = f32(I["L1_ffn_w1"])
    H["g_w2"] = f32(I["L1_ffn_w2"])

    def packt(w, cw):
        K, C = w.shape
        kch, nt = K // 128, C // cw
        return np.ascontiguousarray(
            w.reshape(kch, 128, nt, cw).transpose(1, 2, 0, 3)
             .reshape(128, nt * kch * cw))

    def packr(w, kch, mch):
        K, C = w.shape
        c = C // mch
        return np.ascontiguousarray(
            w.reshape(kch, 128, mch, c).transpose(1, 0, 2, 3)
             .reshape(128, kch * mch * c))

    H["l0_wq"] = packt(H["l0_wq"], 128)
    H["l0_wk"] = packt(H["l0_wk"], 128)
    H["l0_wv"] = packt(H["l0_wv"], 256)
    H["l0_wo"] = packr(H["l0_wo"], 8, 8)
    H["f_wv"] = np.stack([packt(H["f_wqkv"][l][:, 1024:1536], 512)
                          for l in range(2)])
    H["f_wqkv"] = np.stack([packt(H["f_wqkv"][l][:, 0:1024], 128)
                            for l in range(2)])
    H["f_w1"] = np.stack([packt(H["f_w1"][l], 128) for l in range(2)])
    H["f_wo"] = np.stack([packr(H["f_wo"][l], 4, 8) for l in range(2)])
    H["l1_wv"] = packt(H["l1_wvn"][:, 0:1024], 1024)
    H["l1_wn"] = packt(H["l1_wvn"][:, 1024:1056], 32)
    del H["l1_wvn"]
    H["l1_wqk"] = packt(H["l1_wqk"], 128)
    H["l1_wo"] = packt(H["l1_wo"], 128)
    H["g_w1"] = packt(H["g_w1"], 128)

    def cols(v, n):
        v = np.asarray(v, np.float32)
        assert v.size == 128 * n, (v.size, n)
        return v.reshape(n, 128).T

    parts = []
    for name, n in _BIAS_LAYOUT:
        if name == "bq":
            v = np.asarray(I["L0_attn_bq"])[qperm]
        elif name == "bk":
            v = I["L0_attn_bk"]
        elif name == "bo0":
            v = I["L0_attn_bo"]
        elif name in ("qs0", "ks0", "qs1", "ks1"):
            src = I["L0_ffn_q_scale" if name[0] == "q" else "L0_ffn_k_scale"]
            v = np.tile(np.asarray(src)[int(name[2])], 2)
        elif name in ("g0", "g1"):
            v = np.asarray(I["L0_ffn_ln_g"])[int(name[1])]
        elif name in ("lb0", "lb1"):
            v = np.asarray(I["L0_ffn_ln_b"])[int(name[2])]
        elif name in ("b1_0", "b1_1"):
            b1p = np.zeros(2 * HIDP, np.float32)
            b1 = np.asarray(I["L0_ffn_b1"])[int(name[3])]
            b1p[:HID] = b1[:HID]
            b1p[HIDP:HIDP + HID] = b1[HID:]
            v = b1p
        elif name in ("b2_0", "b2_1"):
            v = np.asarray(I["L0_ffn_b2"])[int(name[3])]
        elif name == "bqk":
            v = I["L1_attn_bqk"]
        elif name == "bo1":
            v = I["L1_attn_bo"]
        elif name == "gb1":
            v = I["L1_ffn_b1"]
        elif name == "gb2":
            v = I["L1_ffn_b2"]
        parts.append(cols(v, n))
    H["bias"] = np.ascontiguousarray(np.concatenate(parts, axis=1))
    assert H["bias"].shape[1] == NBIAS

    bvt = np.zeros((128, 1056), np.float32)
    bvt[:, :1024] = np.asarray(I["L1_attn_bv"])[None, :]
    bvt[:, 1024:1040] = np.asarray(I["L1_attn_bn"])[None, :]
    H["l1_bvt"] = bvt
    H["l0_bvt"] = f32(np.broadcast_to(np.asarray(I["L0_attn_bv"])[None, :],
                                      (128, 256)))

    pos = np.arange(TT, dtype=np.float64)
    fr = np.exp(-(np.arange(32, dtype=np.float64) / 32) * math.log(10000.0))
    ang = np.repeat(fr, 2)[:, None] * pos[None, :]
    dup = lambda a: np.ascontiguousarray(
        np.concatenate([a, a], 0).astype(BF))
    cos64, sin64 = np.cos(ang), np.sin(ang)
    H["t_l0a"] = np.stack([dup(cos64 * 0.125), dup(sin64 * 0.125),
                           dup(cos64), dup(sin64)])
    fr2 = 1.0 / 10000.0 ** (np.arange(32, dtype=np.float64) / 32)
    fq = fr2[:, None] * pos[None, :]
    base = (np.arange(0, 64, 2, dtype=np.float64) + 25.6) / 89.6
    power = (pos - TT // 2) / 256.0
    sc = base[:, None] ** power[None, :]
    c2, s2_ = np.cos(fq), np.sin(fq)
    mk = lambda a: dup(np.concatenate([a, a], 0))
    H["t_ffn"] = np.stack([mk(c2 * sc), mk(s2_ * sc),
                           mk(c2 / sc), mk(s2_ / sc)])

    ii = np.arange(128)[:, None]; jj = np.arange(128)[None, :]
    causal = np.where(jj <= ii, 0.0, NEG)
    band = np.where(ii <= jj + 0, 0.0, NEG)
    H["m_causal"] = np.ascontiguousarray(causal.T.astype(BF))
    H["m_band"] = np.ascontiguousarray(band.T.astype(BF))
    H["m_tril"] = np.where(jj >= ii, 1.0, 0.0).astype(BF)
    H["c_id_bf"] = np.eye(128, dtype=BF)
    H["c_id_f32"] = np.eye(128, dtype=np.float32)
    H["c_ones_bf"] = np.ones((128, 128), dtype=BF)
    bd = np.zeros((128, 128), np.float32)
    bd[:64, :64] = 1.0; bd[64:, 64:] = 1.0
    H["c_bdiag_bf"] = bd.astype(BF)
    e16 = np.zeros((128, 1024), np.float32)
    for _k in range(16):
        e16[_k, _k * 64:(_k + 1) * 64] = 1.0
    H["c_blk16"] = e16.astype(BF)
    pp = np.zeros((128, 128), np.float32)
    for i in range(64):
        pp[2 * i + 1, 2 * i] = -1.0
        pp[2 * i, 2 * i + 1] = 1.0
    H["c_permpair"] = pp.astype(BF)
    ph = np.zeros((128, 128), np.float32)
    for b in range(2):
        for i in range(32):
            ph[b * 64 + i + 32, b * 64 + i] = -1.0
            ph[b * 64 + i, b * 64 + i + 32] = 1.0
    H["c_permhalf"] = ph.astype(BF)
    return H


_BIAS_LAYOUT = (("bq", 8), ("bk", 2), ("bo0", 8), ("qs0", 1), ("ks0", 1),
                ("qs1", 1), ("ks1", 1), ("g0", 8), ("lb0", 8), ("g1", 8),
                ("lb1", 8), ("b1_0", 44), ("b1_1", 44), ("b2_0", 8),
                ("b2_1", 8), ("bqk", 16), ("bo1", 8), ("gb1", 32), ("gb2", 8))
NBIAS = sum(n for _, n in _BIAS_LAYOUT)
BIDX = {}
_off = 0
for _nm, _n in _BIAS_LAYOUT:
    BIDX[_nm] = _off
    _off += _n


def _per_core_x(I, c):
    x = np.asarray(I["L0_attn_x"])
    return np.ascontiguousarray(
        np.stack([x[2 * c].T, x[2 * c + 1].T]).astype(np.float32))



class Sem:
    __slots__ = ("name", "h", "n")
    def __init__(self, name):
        self.name, self.h, self.n = name, None, 0


class Bld:
    def __init__(self, nc):
        self.nc = nc
        self.q = {e: [] for e in ENGS}
        self.sems = []
        self.bar = self.sem("bar")
        self.nbar = 0
        self.dirty = {e: False for e in ENGS}

    def sem(self, name):
        s = Sem(name)
        self.sems.append(s)
        return s

    def emit(self, eng, fn, wait=(), inc=None, inc_n=1):
        w = tuple((s, v) for (s, v) in wait if v > 0)
        if inc is not None:
            inc.n += inc_n
            self.dirty[eng] = False
        else:
            self.dirty[eng] = True
        tgt = (inc, inc_n) if inc is not None else None

        def thunk(e, fn=fn, w=w, tgt=tgt):
            for s, v in w:
                e.wait_ge(s.h, v)
            ins = fn(e)
            if tgt is not None:
                if ins is None:
                    ins = e.nop()
                ins.then_inc(tgt[0].h, tgt[1])
        self.q[eng].append(thunk)
        return inc.n if inc is not None else None

    def dma(self, eng, out_ap, in_ap, inc, wait=()):
        n = self.emit(eng, lambda e, o=out_ap, i=in_ap: e.dma_start(out=o, in_=i),
                      wait=wait, inc=inc, inc_n=16)
        self.dirty[eng] = False
        return n

    def barrier(self, dma_sems=(), eng_sems=()):
        self.nbar += 1
        tgt = 5 * self.nbar
        self.bar.n = tgt
        drains = tuple((s, s.n) for s in dma_sems) + \
            tuple((s, s.n) for s in eng_sems)
        for e in ENGS:
            def thunk(en, s=self.bar, t=tgt, dr=drains):
                for ds, dv in dr:
                    if dv > 0:
                        en.wait_ge(ds.h, dv)
                en.nop().then_inc(s.h, 1)
                en.wait_ge(s.h, t)
            self.q[e].append(thunk)

    def finalize(self, stack):
        for s in self.sems:
            s.h = stack.enter_context(self.nc.semaphore(s.name))

    def replay(self):
        nc = self.nc
        with nc.Block() as block:
            @block.tensor
            def _(e):
                for t in self.q["pe"]:
                    t(e)

            @block.scalar
            def _(e):
                for t in self.q["act"]:
                    t(e)

            @block.vector
            def _(e):
                for t in self.q["dve"]:
                    t(e)

            @block.gpsimd
            def _(e):
                for t in self.q["pool"]:
                    t(e)

            @block.sync
            def _(e):
                for t in self.q["sp"]:
                    t(e)



def build_program(debug=(), units=((0, 0), (0, 1), (1, 0), (1, 1)),
                  stages=("l0a", "l0f", "l1"), dump_bufs=()):
    import contextlib
    stack = contextlib.ExitStack()
    nc = bass.Bass("TRN2", target_bir_lowering=False,
                   detect_race_conditions=False)

    dr = {}
    def din(name, shape, dt=F32):
        dr[name] = nc.dram_tensor(name, list(shape), dt, kind="ExternalInput").ap()

    din("xin", (2, DM, TT))
    din("l0_wq", (128, 8192)); din("l0_wk", (128, 2048))
    din("l0_wv", (128, 2048)); din("l0_wo", (128, 8192))
    din("f_wqkv", (2, 128, 8192)); din("f_wv", (2, 128, 4096))
    din("f_wo", (2, 128, 4096))
    din("f_w1", (2, 128, 16 * HIDP)); din("f_w2", (2, HIDP, DM))
    din("l1_wqk", (128, 16384)); din("l1_wv", (128, 8192))
    din("l1_wn", (128, 256)); din("l1_wo", (128, 8192))
    din("g_w1", (128, 32768)); din("g_w2", (4096, DM))
    din("bias", (128, NBIAS))
    din("l1_bvt", (128, 1056)); din("l0_bvt", (128, 256))
    din("t_l0a", (4, 128, TT), BF16); din("t_ffn", (4, 128, TT), BF16)
    din("m_causal", (128, 128), BF16); din("m_band", (128, 128), BF16)
    din("m_tril", (128, 128), BF16)
    din("c_id_bf", (128, 128), BF16); din("c_id_f32", (128, 128))
    din("c_ones_bf", (128, 128), BF16); din("c_bdiag_bf", (128, 128), BF16)
    din("c_permpair", (128, 128), BF16); din("c_permhalf", (128, 128), BF16)
    din("c_blk16", (128, 1024), BF16)
    yout = nc.dram_tensor("yout", [2, DM, TT], F32, kind="ExternalOutput").ap()
    dbg_t = {}
    for nm in debug:
        dbg_t[nm] = nc.dram_tensor(nm, [128, NCH * TH], F32,
                                   kind="ExternalOutput").ap()

    sb = {}
    def sbuf(name, cols, dt=BF16):
        sb[name] = stack.enter_context(nc.sbuf_tensor(name, [128, cols], dt))
        return sb[name]

    X    = sbuf("X", NCH * TH, F32)
    XF   = sbuf("XF", NCH * TH, F32)
    XN   = sbuf("XN", NCH * TH, BF16)
    B1   = sbuf("B1", NCH * TH, BF16)
    B2   = sbuf("B2", NCH * TH, BF16)
    B3   = sbuf("B3", NCH * TH, BF16)
    B4   = sbuf("B4", NCH * TH, BF16)
    KD   = sbuf("KD", 2 * TT, BF16)
    VT0  = sbuf("VT0", 16 * 256, BF16)
    KT   = sbuf("KT", 2 * 2048, BF16)
    VTT  = sbuf("VTT", 2 * 2048, BF16)
    WA   = sbuf("WA", 3 * TH, BF16)
    WB   = sbuf("WB", 2 * TH, BF16)
    TBL  = sbuf("TBL", 2 * TH, BF16)
    MU   = sbuf("MU", TH, F32)
    IV   = sbuf("IV", TH, F32)
    MSB  = sbuf("MSB", 512, F32)
    MBF  = sbuf("MBF", 512, BF16)
    SM   = sbuf("SM", 16, F32)
    BIA  = sbuf("BIA", NBIAS, F32)
    BVT0 = sbuf("BVT0", 256, F32)
    MCS  = sbuf("MCS", 128, BF16)
    MBS  = sbuf("MBS", 128, BF16)
    MT   = sbuf("MT", 128, BF16)
    IDB  = sbuf("IDB", 128, BF16)
    IDF  = sbuf("IDF", 128, F32)
    ONE  = sbuf("ONE", 128, BF16)
    BDG  = sbuf("BDG", 128, BF16)
    PRP  = sbuf("PRP", 128, BF16)
    PRH  = sbuf("PRH", 128, BF16)
    BLK16 = sbuf("BLK16", 1024, BF16)
    LRT  = TBL[:, 0:128].bitcast(F32)
    SHC  = sbuf("SHC", 16, F32)
    NTT  = TBL[:, 128:384].bitcast(F32)
    NRW  = TBL[:, 384:512]
    RST = MU[:, 128:176]
    SQ = B1[:, 4 * TH:6 * TH]

    PS = stack.enter_context(nc.psum_tensor("PS", [128, 4096], F32))
    def bank(i, cols=512, off=0):
        return PS[:, 512 * i + off: 512 * i + off + cols]
    def psym(col0, cols):
        return PS[:, col0:col0 + cols]

    b = Bld(nc)
    s_wdma = b.sem("wdma")
    s_xdma = b.sem("xdma")
    s_out = b.sem("odma")
    s_pe = b.sem("pe")
    s_act = b.sem("act")
    s_dve = b.sem("dve")
    b.finalize(stack)

    def bias_col(name, i=0):
        return BIA[:, BIDX[name] + i: BIDX[name] + i + 1]

    def bar():
        b.barrier((s_xdma, s_wdma, s_out), (s_pe, s_act, s_dve))

    def chs(buf, c, t0=0, n=TH, w=TH):
        return buf[:, c * w + t0: c * w + t0 + n]

    def load_x(seq, half):
        src = dr["xin"][seq, :, half * TH:(half + 1) * TH]
        return b.dma("sp", X[:, :].rearrange("p (c t) -> p c t", c=NCH),
                     src.rearrange("(c p) t -> p c t", p=128), s_xdma)

    def store_x(seq, half, wait):
        dst = yout[seq, :, half * TH:(half + 1) * TH]
        return b.dma("sp", dst.rearrange("(c p) t -> p c t", p=128),
                     X[:, :].rearrange("p (c t) -> p c t", c=NCH),
                     s_out, wait=wait)

    def load_tables(name, t0, t1, half):
        c0 = half * TH
        w = [(s_dve, s_dve.n)]
        b.dma("sp", TBL[:, 0:TH], dr[name][t0, :, c0:c0 + TH], s_xdma, wait=w)
        n = b.dma("sp", TBL[:, TH:2 * TH], dr[name][t1, :, c0:c0 + TH],
                  s_xdma, wait=list(w))
        return n

    PACKT = {"l0_wq": 128, "l0_wk": 128, "l0_wv": 256, "f_wqkv": 128,
             "f_wv": 512, "f_w1": 128, "g_w1": 128, "l1_wqk": 128,
             "l1_wo": 128, "l1_wv": 1024, "l1_wn": 32}

    def wload(dst_ap, w_name, r0, rk, c0, ck, lay=None, wait=()):
        w = dr[w_name] if lay is None else dr[w_name][lay]
        if w_name in PACKT:
            cwp = PACKT[w_name]
            assert ck == cwp and r0 == 0 and c0 % cwp == 0, (w_name, c0, ck)
            t = c0 // cwp
            return b.dma("pool", dst_ap,
                         w[:, t * rk * cwp:(t + 1) * rk * cwp],
                         s_wdma, wait=wait)
        src = w[r0:r0 + rk * 128, c0:c0 + ck]
        src = src.rearrange("(k p) m -> p k m", p=128)
        dst = dst_ap.rearrange("p (k m) -> p k m", k=rk)
        return b.dma("pool", dst, src, s_wdma, wait=wait)

    def ln_stats(src_chunks, nch, scale, banks=(0, 1, 2, 3), src_wait=None,
                 shift_info=None):
        sq_done = [0] * nch
        pe_sq = [0] * nch
        swl = src_wait if src_wait is not None else (lambda c: [])
        for c in range(nch):
            src = src_chunks(c)
            w = list(swl(c)) + ([(s_pe, pe_sq[c - 2])] if c >= 2 else [])
            sq_done[c] = b.emit("act", lambda e, c=c, src=src: e.activation(
                SQ[:, (c % 2) * TH:(c % 2) * TH + TH], src, AF.Square),
                wait=w, inc=s_act)
            for st in range(2):
                b.emit("pe", lambda e, c=c, st=st, src=src: e.matmul(
                    bank(banks[st]), ONE[:, :], src[:, st * 512:(st + 1) * 512],
                    start=(c == 0), stop=(c == nch - 1)),
                    wait=swl(c) if st == 0 else ())
            for st in range(2):
                b.emit("pe", lambda e, c=c, st=st: e.matmul(
                    bank(banks[2 + st]), ONE[:, :],
                    SQ[:, (c % 2) * TH + st * 512:(c % 2) * TH + (st + 1) * 512],
                    start=(c == 0), stop=(c == nch - 1)),
                    wait=[(s_act, sq_done[c])] if st == 0 else (),
                    inc=s_pe if st == 1 else None)
            pe_sq[c] = s_pe.n
        pe_done = s_pe.n
        b.emit("dve", lambda e: e.tensor_scalar_mul(
            MU[:, :], bank(banks[0], 1024), scale), wait=[(s_pe, pe_done)],
            inc=s_dve)
        b.emit("dve", lambda e: e.tensor_mul(IV[:, :], MU[:, :], MU[:, :]),
               inc=s_dve)
        b.emit("dve", lambda e: e.scalar_tensor_tensor(
            IV[:, :], bank(banks[2], 1024), scale, IV[:, :], ALU.mult,
            ALU.subtract), inc=s_dve)
        b.emit("dve", lambda e: e.tensor_single_scalar(
            IV[:, :], IV[:, :], 1e-5, ALU.add), inc=s_dve)
        dve_stats = s_dve.n
        b.emit("act", lambda e: e.activation(IV[:, :], IV[:, :], AF.Ln),
               wait=[(s_dve, dve_stats)], inc=s_act)
        act_stats = b.emit("act", lambda e: e.activation(
            IV[:, :], IV[:, :], AF.Exp, scale=-0.5), inc=s_act)
        return dve_stats, act_stats

    def ln_top(x_ready):
        cast_done = [0] * NCH
        for c in range(NCH):
            cast_done[c] = b.emit("dve", lambda e, c=c: e.tensor_copy(
                chs(XN, c), chs(X, c)),
                wait=[(s_xdma, x_ready)] if c == 0 else (), inc=s_dve)
        _, act_stats = ln_stats(lambda c: chs(XN, c), NCH, 1.0 / DM,
                                src_wait=lambda c: [(s_dve, cast_done[c])])
        for c in range(NCH):
            b.emit("dve", lambda e, c=c: e.tensor_sub(
                chs(XN, c), chs(XN, c), MU[:, :]),
                wait=[(s_act, act_stats)] if c == 0 else (), inc=s_dve)
            b.emit("dve", lambda e, c=c: e.tensor_mul(
                chs(XN, c), chs(XN, c), IV[:, :]), inc=s_dve)
        return s_dve.n

    def proj_fmaj(wname, lay, din_, dout, rhs_fn, evac, Tcols=TH,
                  wslots=None, mt0=0, mtn=None, rhs_wait=None, c0=0, cw=128):
        kch = din_ // 128
        mtn = mtn if mtn is not None else dout // 128
        slots = wslots or [0, 1, 2]
        ns = len(slots)
        nstr = max(1, Tcols // 512)
        rw = rhs_wait if rhs_wait is not None else (lambda k: [])
        wld = {}
        mm_done = {}
        ev_gate = {}
        mlist = list(range(mt0, mt0 + mtn))
        for i, m in enumerate(mlist[:ns]):
            wld[m] = wload(WA[:, slots[i] * TH:slots[i] * TH + kch * 128],
                           wname, 0, kch, c0 + m * cw, cw, lay=lay,
                           wait=[(s_pe, s_pe.n)] if i == 0 else ())
        for mi, m in enumerate(mlist):
            sl = slots[mi % ns]
            bs = (mi % 2) * 2
            for k in range(kch):
                lhs = WA[:, sl * TH + k * 128: sl * TH + (k + 1) * 128]
                for st in range(nstr):
                    wl = []
                    if k == 0 and st == 0:
                        wl.append((s_wdma, wld[m]))
                        if mi - 2 >= 0 and mlist[mi - 2] in ev_gate:
                            wl.append(ev_gate[mlist[mi - 2]])
                    wl += rw(k)
                    b.emit("pe", lambda e, lhs=lhs, k=k, st=st, bs=bs,
                           rhs_fn=rhs_fn: e.matmul(
                        bank(bs + st, min(512, Tcols)), lhs,
                        rhs_fn(k)[:, st * 512:st * 512 + min(512, Tcols)],
                        start=(k == 0), stop=(k == kch - 1)),
                        wait=wl,
                        inc=s_pe if (k == kch - 1 and st == nstr - 1) else None)
            mm_done[m] = s_pe.n
            ev_gate[m] = evac(m, bank(bs, Tcols), mm_done[m])
            if mi + ns < mtn:
                m2 = mlist[mi + ns]
                wld[m2] = wload(WA[:, sl * TH:sl * TH + kch * 128], wname,
                                0, kch, c0 + m2 * cw, cw, lay=lay,
                                wait=[(s_pe, mm_done[m])])
        return s_pe.n

    def rope(src_ap, dst_ap, perm, cos_ap, sin_ap, rot_bank, scr_ap,
             src_ready, label=""):
        for st in range(2):
            b.emit("pe", lambda e, st=st: e.matmul(
                bank(rot_bank + st, 512), perm[:, :],
                src_ap[:, st * 512:(st + 1) * 512], start=True, stop=True),
                wait=list(src_ready) if st == 0 else (),
                inc=s_pe if st == 1 else None)
        mmc = s_pe.n
        a = b.emit("act", lambda e: e.copy(scr_ap, bank(rot_bank, 1024)),
                   wait=[(s_pe, mmc)], inc=s_act)
        d1 = b.emit("dve", lambda e: e.tensor_mul(dst_ap, scr_ap, sin_ap),
                    wait=[(s_act, a)], inc=s_dve)
        b.emit("dve", lambda e: e.tensor_mul(scr_ap, src_ap, cos_ap),
               inc=s_dve)
        d2 = b.emit("dve", lambda e: e.tensor_add(dst_ap, dst_ap, scr_ap),
                    inc=s_dve)
        return mmc, d2

    def attn_softmax(qt, hp_list, q_ap_fn, k_ap_fn, tiles, v_ap_fn,
                     ctx_out_fn, pt_sb, p_sb, extra_score_wait=()):
        nt = len(tiles)
        prev_exp_all = [s_act.n]
        prev_ctx_read = [s_dve.n]
        for hp in hp_list:
            for h64 in range(2):
                b0 = 0 if h64 == 0 else 4
                tp = (0, 0) if h64 == 0 else (64, 0)
                for ti, (stile, msk) in enumerate(tiles):
                    bk = b0 + ti // 4
                    off = (ti % 4) * 128
                    wl = []
                    if ti == 0 and h64 == 0:
                        wl += list(extra_score_wait)
                        wl.append((s_dve, prev_ctx_read[0]))
                    if msk is not None:
                        mm = MCS if msk == 'causal' else MBS
                        b.emit("pe", lambda e, bk=bk, off=off, mm=mm:
                               e.matmul(bank(bk, 128, off), mm[:, :],
                                        IDB[:, :], start=True, stop=False),
                               wait=wl)
                        wl = []
                    b.emit("pe", lambda e, bk=bk, off=off, hp=hp, h64=h64,
                           stile=stile, tp=tp, acc=(msk is not None):
                           e.matmul(bank(bk, 128, off), q_ap_fn(hp, h64),
                                    k_ap_fn(hp, stile, h64),
                                    start=(not acc), stop=True,
                                    tile_position=tp),
                           wait=wl,
                           inc=s_pe if (ti == nt - 1) else None)
            pe_scores = s_pe.n
            ncols = nt * 128
            npass = (ncols + 1023) // 1024
            buf_exp = {0: prev_exp_all[0], 1: prev_exp_all[0]}
            for h64 in range(2):
                b0 = 0 if h64 == 0 else 4
                b.emit("dve", lambda e, b0=b0, ncols=ncols, h64=h64:
                       e.reduce_max(SM[:, h64:h64 + 1],
                                    psym(b0 * 512, ncols), axis=AX.X),
                       wait=[(s_pe, pe_scores)] if h64 == 0 else (),
                       inc=s_dve)
                mx_done = s_dve.n
                for pz in range(npass):
                    cc = pz * 1024
                    cn = min(ncols - cc, 1024)
                    bix = h64 if npass == 1 else pz
                    sfp_buf = MU if bix == 0 else IV
                    b.emit("dve", lambda e, b0=b0, cc=cc, cn=cn,
                           sfp_buf=sfp_buf, h64=h64:
                           e.tensor_single_scalar(
                               sfp_buf[:, 0:cn], psym(b0 * 512 + cc, cn),
                               SM[:, h64:h64 + 1], ALU.subtract),
                           wait=[(s_dve, mx_done), (s_act, buf_exp[bix])],
                           inc=s_dve)
                    sd = s_dve.n
                    buf_exp[bix] = b.emit(
                        "act", lambda e, cn=cn, sfp_buf=sfp_buf, h64=h64, pz=pz:
                        e.activation(
                            p_sb[:, h64 * 2048 + pz * 1024:
                                 h64 * 2048 + pz * 1024 + cn],
                            sfp_buf[:, 0:cn], AF.Exp,
                            accum_out=SM[:, 4 + 2 * h64 + pz:
                                         5 + 2 * h64 + pz]),
                        wait=[(s_dve, sd)], inc=s_act)
            b.emit("act", lambda e: e.copy(SM[:, 12:13], SM[:, 4:5]),
                   inc=s_act)
            exp_all = s_act.n
            sub_all = s_dve.n
            prev_exp_all[0] = exp_all
            for h64 in range(2):
                if npass == 2:
                    b.emit("dve", lambda e, h64=h64: e.tensor_add(
                        SM[:, 8 + h64:9 + h64],
                        SM[:, 4 + 2 * h64:5 + 2 * h64],
                        SM[:, 5 + 2 * h64:6 + 2 * h64]),
                        wait=[(s_act, exp_all)] if h64 == 0 else (),
                        inc=s_dve)
                else:
                    b.emit("dve", lambda e, h64=h64: e.tensor_copy(
                        SM[:, 8 + h64:9 + h64],
                        SM[:, 4 + 2 * h64:5 + 2 * h64]),
                        wait=[(s_act, exp_all)] if h64 == 0 else (),
                        inc=s_dve)
            lsum_d = s_dve.n
            b.emit("dve", lambda e: e.reciprocal(SM[:, 10:11], SM[:, 8:9]),
                   wait=[(s_dve, lsum_d)], inc=s_dve)
            b.emit("dve", lambda e: e.reciprocal(SM[:, 11:12], SM[:, 9:10]),
                   inc=s_dve)
            invl_done = s_dve.n
            tr_slot = {}
            tr_recent = []
            for h64 in range(2):
                cp = [0] * nt
                for ti in range(nt):
                    si = (h64 * nt + ti) % 8
                    tb_ap = psym((si % 2) * 512 + (si // 2) * 64,
                                 64).bitcast(BF16)
                    wl = []
                    if h64 == 0 and ti == 0:
                        wl += [(s_act, exp_all), (s_dve, sub_all)]
                    if len(tr_recent) >= 2:
                        wl.append((s_dve, tr_recent[-2]))
                    if si in tr_slot:
                        wl.append((s_dve, tr_slot[si]))
                    b.emit("pe", lambda e, h64=h64, ti=ti, tb_ap=tb_ap:
                           e.transpose(tb_ap,
                                       p_sb[:, h64 * 2048 + ti * 128:
                                            h64 * 2048 + (ti + 1) * 128],
                                       IDB[:, :]),
                           wait=wl, inc=s_pe)
                    trc = s_pe.n
                    cp[ti] = b.emit("dve", lambda e, h64=h64, ti=ti,
                                    tb_ap=tb_ap, nt=nt: e.tensor_copy(
                                        pt_sb[:, (h64 * nt + ti) * 128:
                                              (h64 * nt + ti + 1) * 128],
                                        tb_ap),
                                    wait=[(s_pe, trc)], inc=s_dve)
                    tr_slot[si] = cp[ti]
                    tr_recent.append(cp[ti])
                for ti in range(nt):
                    st_tile = tiles[ti][0]
                    pv_wl = [(s_dve, cp[ti])]
                    if h64 == 0 and ti == 0:
                        pv_wl.append((s_dve, prev_ctx_read[0]))
                    b.emit("pe", lambda e, h64=h64, ti=ti, hp=hp, nt=nt,
                           st_tile=st_tile:
                           e.matmul(bank(2, 64, h64 * 64),
                                    pt_sb[:, (h64 * nt + ti) * 128:
                                          (h64 * nt + ti + 1) * 128],
                                    v_ap_fn(hp, st_tile, h64),
                                    start=(ti == 0), stop=(ti == nt - 1)),
                           wait=pv_wl,
                           inc=s_pe if (h64 == 1 and ti == nt - 1) else None)
            ctxmm = s_pe.n
            dst, post = ctx_out_fn(hp)
            for h64 in range(2):
                b.emit("dve", lambda e, h64=h64, dst=dst:
                       e.tensor_single_scalar(
                           dst[:, h64 * 64:(h64 + 1) * 64],
                           bank(2, 64, h64 * 64),
                           SM[:, 10 + h64:11 + h64], ALU.mult),
                       wait=[(s_pe, ctxmm), (s_dve, invl_done)], inc=s_dve)
            prev_ctx_read[0] = s_dve.n
            if post is not None:
                post()

    def stage_l0_attn(seq, half, xn_ready):
        c0 = half * TH
        wo_ld = b.dma("pool", B3[:, :], dr["l0_wo"][:, :], s_wdma)
        tb = load_tables("t_l0a", 2, 3, half)
        def k_evac(m, ps, pe_cnt):
            a = b.emit("act", lambda e, m=m, ps=ps: e.activation(
                B1[:, 0:TH], ps, AF.Identity, bias=bias_col("bk", m)),
                wait=[(s_pe, pe_cnt), (s_xdma, tb), (s_dve, s_dve.n)],
                inc=s_act)
            rope(B1[:, 0:TH], KD[:, m * TT + c0:m * TT + c0 + TH], PRP,
                 TBL[:, 0:TH], TBL[:, TH:2 * TH], 4,
                 B1[:, TH:2 * TH], [(s_act, a)])
            return (s_act, a)
        proj_fmaj("l0_wk", None, DM, 256, lambda k: chs(XN, k), k_evac,
                  rhs_wait=lambda k: [(s_dve, xn_ready)] if k == 0 else [])
        wv_ld = wload(WB[:, 0:2048], "l0_wv", 0, 8, 0, 256,
                      wait=[(s_pe, s_pe.n)])
        for s in range(8):
            for k in range(8):
                b.emit("pe", lambda e, s=s, k=k: e.matmul(
                    bank(6, 256), chs(XN, k)[:, s * 128:(s + 1) * 128],
                    WB[:, k * 256:(k + 1) * 256],
                    start=(k == 0), stop=(k == 7)),
                    wait=([(s_wdma, wv_ld), (s_dve, xn_ready)]
                          if (s == 0 and k == 0) else
                          ([(s_dve, s_dve.n)] if k == 0 else ())),
                    inc=s_pe if k == 7 else None)
            vp = s_pe.n
            b.emit("dve", lambda e, s=s: e.tensor_add(
                VT0[:, (half * 8 + s) * 256:(half * 8 + s + 1) * 256],
                bank(6, 256), BVT0[:, :]), wait=[(s_pe, vp)], inc=s_dve)
        tq = load_tables("t_l0a", 0, 1, half)
        def q_evac(m, ps, pe_cnt):
            a = b.emit("act", lambda e, m=m, ps=ps: e.activation(
                B1[:, 0:TH], ps, AF.Identity, bias=bias_col("bq", m)),
                wait=[(s_pe, pe_cnt), (s_xdma, tq), (s_dve, s_dve.n)],
                inc=s_act)
            rope(B1[:, 0:TH], chs(B2, m), PRP,
                 TBL[:, 0:TH], TBL[:, TH:2 * TH], 4,
                 B1[:, TH:2 * TH], [(s_act, a)])
            return (s_act, a)
        proj_fmaj("l0_wq", None, DM, DM, lambda k: chs(XN, k), q_evac)
        prep_done_dve = s_dve.n
        prep_done_pe = s_pe.n

        def k_ap(hp, stile, h64):
            kc = 0 if hp < 4 else 1
            return KD[h64 * 64:(h64 + 1) * 64,
                      kc * TT + stile * 128: kc * TT + (stile + 1) * 128]

        def v_ap(hp, stile, h64):
            g = QHEADS[2 * hp + h64] // 4
            return VT0[:, stile * 256 + g * 64: stile * 256 + (g + 1) * 64]

        prev_wo_evac = [s_dve.n]
        prev_ctxf_evac = [s_act.n]
        import os
        qtr = range(int(os.environ.get('QT0', 0)), int(os.environ.get('QT1', 8)))
        for qt in qtr:
            qg = half * 8 + qt
            tiles = [(s, 'causal' if s == qg else None) for s in range(qg + 1)]

            def q_ap(hp, h64, qt=qt):
                return B2[h64 * 64:(h64 + 1) * 64,
                          hp * TH + qt * 128: hp * TH + (qt + 1) * 128]

            def ctx_out(hp, qt=qt):
                dst = B1[:, TH + (hp % 2) * 128: TH + (hp % 2 + 1) * 128]
                def post(hp=hp, dst=dst):
                    td = s_dve.n
                    sl = 3 * 512 + (hp % 2) * 64
                    wl = [(s_dve, td)]
                    if hp >= 2:
                        wl.append((s_act, prev_ctxf_evac[0]))
                    b.emit("pe", lambda e, dst=dst, sl=sl: e.transpose(
                        psym(sl, 64).bitcast(BF16), dst, IDB[:, :]),
                        wait=wl, inc=s_pe)
                    tc = s_pe.n
                    prev_ctxf_evac[0] = b.emit(
                        "act", lambda e, hp=hp, sl=sl: e.copy(
                            B1[:, 2 * TH + hp * 128: 2 * TH + (hp + 1) * 128],
                            psym(sl, 64).bitcast(BF16)),
                        wait=[(s_pe, tc)], inc=s_act)
                return dst, post

            attn_softmax(qt, list(range(8)), q_ap, k_ap, tiles, v_ap,
                         ctx_out, pt_sb=B1[:, 3 * TH:7 * TH],
                         p_sb=B4[:, 0:4096],
                         extra_score_wait=[(s_dve, max(prep_done_dve,
                                                       prev_wo_evac[0]))])
            ctxf_done = s_act.n
            wo_ev = {}
            for m in range(8):
                wbk = 6 + (m % 2)
                woff = (m // 2 % 2) * 256 + (m // 4) * 128
                for k in range(8):
                    wl = []
                    if m == 0 and k == 0:
                        wl = [(s_act, ctxf_done), (s_wdma, wo_ld)]
                    elif k == 0 and m >= 2:
                        wl = [(s_dve, wo_ev[m - 2])]
                    b.emit("pe", lambda e, m=m, k=k, wbk=wbk, woff=woff:
                           e.matmul(bank(wbk, 128, woff),
                                    B3[:, (k * 8 + m) * 128:(k * 8 + m + 1) * 128],
                                    B1[:, 2 * TH + k * 128:2 * TH + (k + 1) * 128],
                                    start=(k == 0), stop=(k == 7)),
                           wait=wl,
                           inc=s_pe if k == 7 else None)
                wc = s_pe.n
                wo_ev[m] = b.emit(
                    "dve", lambda e, m=m, qt=qt, wbk=wbk, woff=woff:
                    e.scalar_tensor_tensor(
                        chs(X, m, qt * 128, 128), bank(wbk, 128, woff),
                        bias_col("bo0", m), chs(X, m, qt * 128, 128),
                        ALU.add, ALU.add),
                    wait=[(s_pe, wc)], inc=s_dve)
            prev_wo_evac[0] = s_dve.n

    def stage_l0_ffn(seq, half, xn_ready):
        XSH = B2[:, 4 * TH:8 * TH]
        xf0 = 0
        for c in range(NCH):
            xf0 = b.emit("dve", lambda e, c=c: e.tensor_copy(
                chs(XF, c), chs(XN, c)),
                wait=[(s_dve, xn_ready)] if c == 0 else (), inc=s_dve)

        for ly in range(2):
            src = XN
            if ly == 1:
                for c in range(NCH):
                    b.emit("dve", lambda e, c=c: e.tensor_copy(
                        chs(XN, c), chs(XF, c)), inc=s_dve)
            for c in range(4):
                if half == 0:
                    b.emit("dve", lambda e, c=c: e.memset(
                        XSH[:, c * TH:c * TH + 1], 0.0), inc=s_dve)
                else:
                    b.emit("dve", lambda e, c=c, ly=ly: e.tensor_copy(
                        XSH[:, c * TH:c * TH + 1],
                        SHC[:, ly * 4 + c:ly * 4 + c + 1]), inc=s_dve)
                b.emit("dve", lambda e, c=c: e.tensor_copy(
                    XSH[:, c * TH + 1:(c + 1) * TH],
                    chs(src, c + 4)[:, 0:TH - 1]), inc=s_dve)
            if half == 0:
                for c in range(4):
                    b.emit("dve", lambda e, c=c, ly=ly: e.tensor_copy(
                        SHC[:, ly * 4 + c:ly * 4 + c + 1],
                        chs(src, c + 4)[:, TH - 1:TH]), inc=s_dve)
            shifts_done = s_dve.n
            rhs_c = lambda k: (chs(src, k) if k < 4 else
                               XSH[:, (k - 4) * TH:(k - 4 + 1) * TH])

            tb_box = [load_tables("t_ffn", 0, 1, half)]
            qfin_gate = [0]

            def qk_evac(m, ps, pe_cnt, ly=ly):
                isq = m < 4
                hp = m if isq else m - 4
                if m == 4:
                    tb_box[0] = load_tables("t_ffn", 2, 3, half)
                a = b.emit("act", lambda e, ps=ps: e.activation(
                    B1[:, 0:TH], ps, AF.Identity),
                    wait=[(s_pe, pe_cnt), (s_xdma, tb_box[0]),
                          (s_dve, s_dve.n)],
                    inc=s_act)
                sqd = b.emit("act", lambda e: e.activation(
                    B1[:, 4 * TH:5 * TH], B1[:, 0:TH], AF.Square), inc=s_act)
                for st in range(2):
                    b.emit("pe", lambda e, st=st: e.matmul(
                        bank(6 + st), BDG[:, :],
                        B1[:, 4 * TH + st * 512:4 * TH + (st + 1) * 512],
                        start=True, stop=True),
                        wait=[(s_act, sqd), (s_dve, qfin_gate[0])]
                        if st == 0 else (),
                        inc=s_pe if st == 1 else None)
                ssd = s_pe.n
                b.emit("act", lambda e: e.activation(
                    MU[:, :], psym(6 * 512, 1024), AF.Ln),
                    wait=[(s_pe, ssd)], inc=s_act)
                ivd = b.emit("act", lambda e: e.activation(
                    MU[:, :], MU[:, :], AF.Exp, scale=-0.5), inc=s_act)
                sc = bias_col("qs%d" % ly) if isq else bias_col("ks%d" % ly)
                if isq:
                    b.emit("dve", lambda e, sc=sc: e.tensor_scalar(
                        B1[:, TH:2 * TH], B1[:, 0:TH], sc, 8.0,
                        ALU.mult, ALU.mult), wait=[(s_act, a)], inc=s_dve)
                else:
                    b.emit("dve", lambda e, sc=sc: e.tensor_scalar_mul(
                        B1[:, TH:2 * TH], B1[:, 0:TH], sc),
                        wait=[(s_act, a)], inc=s_dve)
                _, rd = rope(B1[:, TH:2 * TH], B1[:, 2 * TH:3 * TH], PRH,
                             TBL[:, 0:TH], TBL[:, TH:2 * TH], 4,
                             B1[:, 3 * TH:4 * TH], [(s_dve, s_dve.n)])
                dst = (chs(B2, hp) if isq else chs(B3, hp))
                qfin_gate[0] = b.emit(
                    "dve", lambda e, dst=dst: e.tensor_mul(
                        dst, B1[:, 2 * TH:3 * TH], MU[:, :]),
                    wait=[(s_act, ivd)], inc=s_dve)
                return (s_act, ivd)

            proj_fmaj("f_wqkv", ly, DM, DM, rhs_c, qk_evac,
                      rhs_wait=lambda k: [(s_dve, shifts_done)] if k == 0 else [])
            vld = wload(B4[:, 4096:8192], "f_wv", 0, 8, 0, 512, lay=ly,
                        wait=[(s_pe, s_pe.n)])
            for s in range(8):
                for k in range(8):
                    b.emit("pe", lambda e, s=s, k=k: e.matmul(
                        bank(2, 512), rhs_c(k)[:, s * 128:(s + 1) * 128],
                        B4[:, 4096 + k * 512:4096 + (k + 1) * 512],
                        start=(k == 0), stop=(k == 7)),
                        wait=([(s_wdma, vld), (s_act, s_act.n),
                               (s_dve, s_dve.n)] if (s == 0 and k == 0) else
                              ([(s_dve, s_dve.n)] if k == 0 else ())),
                        inc=s_pe if k == 7 else None)
                vp = s_pe.n
                b.emit("dve", lambda e, s=s: e.tensor_copy(
                    B4[:, s * 512:(s + 1) * 512], bank(2, 512)),
                    wait=[(s_pe, vp)], inc=s_dve)
            vt_done = s_dve.n
            if half == 0:
                for c in range(4):
                    b.emit("dve", lambda e, c=c, ly=ly: e.tensor_copy(
                        KT[:, ly * 2048 + c * 512:ly * 2048 + (c + 1) * 512],
                        chs(B3, c)[:, 512:1024]), inc=s_dve)
                b.emit("dve", lambda e, ly=ly: e.tensor_copy(
                    VTT[:, ly * 2048:(ly + 1) * 2048], B4[:, 2048:4096]),
                    inc=s_dve)
            prep = s_dve.n

            def k_ap(hp, stile, h64, ly=ly):
                if stile >= half * 8:
                    sl = stile - half * 8
                    return chs(B3, hp)[h64 * 64:(h64 + 1) * 64,
                                       sl * 128:(sl + 1) * 128]
                base = ly * 2048 + hp * 512
                off = (stile - 4) * 128
                return KT[h64 * 64:(h64 + 1) * 64, base + off:base + off + 128]

            def v_ap(hp, stile, h64, ly=ly):
                h = 2 * hp + h64
                if stile >= half * 8:
                    sl = stile - half * 8
                    return B4[:, sl * 512 + h * 64:sl * 512 + (h + 1) * 64]
                base = ly * 2048 + (stile - 4) * 512
                return VTT[:, base + h * 64:base + (h + 1) * 64]

            prev_wo_evac = [s_dve.n]
            prev_ctxf_evac = [s_act.n]
            wo_ld = b.dma("pool", B3[:, 4096:8192], dr["f_wo"][ly, :, :],
                          s_wdma, wait=[(s_pe, s_pe.n)])
            for qt in range(8):
                qg = half * 8 + qt
                s_lo = max(0, qg - 4)
                tiles = [(s, ('causal' if s == qg else
                              ('band' if (s == qg - 4) else None)))
                         for s in range(s_lo, qg + 1)]

                def q_ap(hp, h64, qt=qt):
                    return B2[h64 * 64:(h64 + 1) * 64,
                              hp * TH + qt * 128:hp * TH + (qt + 1) * 128]

                def ctx_out(hp, qt=qt):
                    dst = B1[:, TH + (hp % 2) * 128:TH + (hp % 2 + 1) * 128]
                    def post(hp=hp, dst=dst):
                        td = s_dve.n
                        sl = 3 * 512 + (hp % 2) * 64
                        wl = [(s_dve, td)]
                        if hp >= 2:
                            wl.append((s_act, prev_ctxf_evac[0]))
                        b.emit("pe", lambda e, dst=dst, sl=sl: e.transpose(
                            psym(sl, 64).bitcast(BF16), dst, IDB[:, :]),
                            wait=wl, inc=s_pe)
                        tc = s_pe.n
                        prev_ctxf_evac[0] = b.emit(
                            "act", lambda e, hp=hp, sl=sl: e.copy(
                                B1[:, 2 * TH + hp * 128:2 * TH + (hp + 1) * 128],
                                psym(sl, 64).bitcast(BF16)),
                            wait=[(s_pe, tc)], inc=s_act)
                    return dst, post

                attn_softmax(qt, list(range(4)), q_ap, k_ap, tiles, v_ap,
                             ctx_out, pt_sb=B1[:, 3 * TH:7 * TH],
                             p_sb=B4[:, 4096:8192],
                             extra_score_wait=[(s_dve, max(prep,
                                                           prev_wo_evac[0]))])
                ctxf_done = s_act.n
                wo_ev = {}
                for m in range(8):
                    wbk = 6 + (m % 2)
                    woff = (m // 2 % 2) * 256 + (m // 4) * 128
                    for k in range(4):
                        wl = []
                        if m == 0 and k == 0:
                            wl = [(s_act, ctxf_done), (s_wdma, wo_ld)]
                        elif k == 0 and m >= 2:
                            wl = [(s_dve, wo_ev[m - 2])]
                        b.emit("pe", lambda e, m=m, k=k, wbk=wbk, woff=woff:
                               e.matmul(bank(wbk, 128, woff),
                                        B3[:, 4096 + (k * 8 + m) * 128:
                                           4096 + (k * 8 + m + 1) * 128],
                                        B1[:, 2 * TH + k * 128:
                                           2 * TH + (k + 1) * 128],
                                        start=(k == 0), stop=(k == 3)),
                               wait=wl, inc=s_pe if k == 3 else None)
                    wc = s_pe.n
                    wo_ev[m] = b.emit(
                        "dve", lambda e, m=m, qt=qt, wbk=wbk, woff=woff:
                        e.tensor_add(chs(XF, m, qt * 128, 128),
                                     bank(wbk, 128, woff),
                                     chs(XF, m, qt * 128, 128)),
                        wait=[(s_pe, wc)], inc=s_dve)
                prev_wo_evac[0] = s_dve.n

            for c in range(NCH):
                b.emit("dve", lambda e, c=c: e.tensor_copy(
                    chs(XN, c), chs(XF, c)), inc=s_dve)
            for c in range(4):
                if half == 0:
                    b.emit("dve", lambda e, c=c: e.memset(
                        XSH[:, c * TH:c * TH + 1], 0.0), inc=s_dve)
                else:
                    b.emit("dve", lambda e, c=c, ly=ly: e.tensor_copy(
                        XSH[:, c * TH:c * TH + 1],
                        SHC[:, 8 + ly * 4 + c:8 + ly * 4 + c + 1]), inc=s_dve)
                b.emit("dve", lambda e, c=c: e.tensor_copy(
                    XSH[:, c * TH + 1:(c + 1) * TH],
                    chs(XN, c + 4)[:, 0:TH - 1]), inc=s_dve)
            if half == 0:
                for c in range(4):
                    b.emit("dve", lambda e, c=c, ly=ly: e.tensor_copy(
                        SHC[:, 8 + ly * 4 + c:8 + ly * 4 + c + 1],
                        chs(XN, c + 4)[:, TH - 1:TH]), inc=s_dve)
            cast2 = s_dve.n
            ln2src = lambda c: (chs(XN, c) if c < 4 else
                                XSH[:, (c - 4) * TH:(c - 4 + 1) * TH])
            _, act_st = ln_stats(ln2src, NCH, 1.0 / DM,
                                 src_wait=lambda c: [(s_dve, cast2)])
            gname, bname = "g%d" % ly, "lb%d" % ly
            for c in range(NCH):
                b.emit("dve", lambda e, c=c: e.tensor_sub(
                    chs(B2, c), ln2src(c), MU[:, :]),
                    wait=[(s_act, act_st)] if c == 0 else (), inc=s_dve)
                b.emit("dve", lambda e, c=c: e.tensor_mul(
                    chs(B2, c), chs(B2, c), IV[:, :]), inc=s_dve)
                b.emit("dve", lambda e, c=c, gname=gname, bname=bname:
                       e.tensor_scalar(chs(B2, c), chs(B2, c),
                                       bias_col(gname, c), bias_col(bname, c),
                                       ALU.mult, ALU.add), inc=s_dve)
            hn_done = s_dve.n

            ngrp = (22 + 7) // 8
            w2_out_gate = [(s_dve, s_dve.n)]
            for g in range(ngrp):
                jlist = list(range(g * 8, min(22, (g + 1) * 8)))
                hg_done = [0]
                ev_prev = {}
                for ji, j in enumerate(jlist):
                    sl = ji % 3
                    bs = (ji % 2) * 2
                    gld = wload(WA[:, sl * TH:sl * TH + 1024], "f_w1", 0, 8,
                                2816 + j * 128, 128, lay=ly,
                                wait=[(s_pe, s_pe.n)] if ji < 3 else
                                [(s_pe, ev_prev[jlist[ji - 3]][1])])
                    for k in range(NCH):
                        for st in range(2):
                            wl = []
                            if k == 0 and st == 0:
                                wl.append((s_wdma, gld))
                                if ji == 0 and g == 0:
                                    wl.append((s_dve, hn_done))
                                if ji - 2 >= 0 and jlist[ji - 2] in ev_prev:
                                    wl.append(ev_prev[jlist[ji - 2]][0])
                            b.emit("pe", lambda e, k=k, st=st, bs=bs, sl=sl:
                                   e.matmul(bank(bs + st),
                                            WA[:, sl * TH + k * 128:
                                               sl * TH + (k + 1) * 128],
                                            chs(B2, k)[:, st * 512:
                                                       (st + 1) * 512],
                                            start=(k == 0), stop=(k == 7)),
                                   wait=wl,
                                   inc=s_pe if (k == 7 and st == 1) else None)
                    gmm = s_pe.n
                    ga = b.emit("act", lambda e, bs=bs, j=j, ly=ly:
                                e.activation(B1[:, 0:TH], bank(bs, 1024),
                                             AF.Gelu,
                                             bias=bias_col("b1_%d" % ly,
                                                           22 + j)),
                                wait=[(s_pe, gmm)], inc=s_act)
                    bs2 = 4 + (ji % 2) * 2
                    ald = wload(WA[:, sl * TH + 1024:sl * TH + 1024 + 1024]
                                if False else WB[:, (ji % 2) * 1024:
                                                 (ji % 2) * 1024 + 1024],
                                "f_w1", 0, 8, j * 128, 128, lay=ly,
                                wait=[(s_pe, s_pe.n)])
                    for k in range(NCH):
                        for st in range(2):
                            wl = []
                            if k == 0 and st == 0:
                                wl.append((s_wdma, ald))
                                if ji - 2 >= 0 and jlist[ji - 2] in ev_prev:
                                    wl.append(ev_prev[jlist[ji - 2]][2])
                            b.emit("pe", lambda e, k=k, st=st, bs2=bs2, ji=ji:
                                   e.matmul(bank(bs2 + st),
                                            WB[:, (ji % 2) * 1024 + k * 128:
                                               (ji % 2) * 1024 + (k + 1) * 128],
                                            chs(B2, k)[:, st * 512:
                                                       (st + 1) * 512],
                                            start=(k == 0), stop=(k == 7)),
                                   wait=wl,
                                   inc=s_pe if (k == 7 and st == 1) else None)
                    amm = s_pe.n
                    hg_done[0] = b.emit(
                        "dve", lambda e, bs2=bs2, j=j, ji=ji, ly=ly:
                        e.scalar_tensor_tensor(
                            B3[:, ji * TH:(ji + 1) * TH], bank(bs2, 1024),
                            bias_col("b1_%d" % ly, j), B1[:, 0:TH],
                            ALU.add, ALU.mult),
                        wait=[(s_pe, amm), (s_act, ga)], inc=s_dve)
                    ev_prev[j] = ((s_act, ga), gmm, (s_dve, hg_done[0]))
                hg_all = s_dve.n
                for mh in range(2):
                    for st in range(2):
                        w2e_gate = w2_out_gate[0]
                        jcnt = {}
                        for ji, j in enumerate(jlist):
                            w2ld = wload(WA[:, (ji % 3) * TH:
                                             (ji % 3) * TH + 1024],
                                         "f_w2", (g * 8 + ji) * 128,
                                         1, 0, 1024, lay=ly,
                                         wait=[(s_pe, jcnt[ji - 3])]
                                         if ji >= 3 else [(s_pe, s_pe.n)])
                            for mi in range(4):
                                m = mh * 4 + mi
                                wl = []
                                if mi == 0:
                                    wl.append((s_wdma, w2ld))
                                    wl.append((s_dve, hg_all))
                                    if ji == 0:
                                        wl.append(w2e_gate)
                                b.emit("pe", lambda e, ji=ji, mi=mi, m=m,
                                       st=st, j=j, nj=len(jlist):
                                       e.matmul(bank(4 + mi),
                                                WA[:, (ji % 3) * TH + m * 128:
                                                   (ji % 3) * TH + (m + 1) * 128],
                                                B3[:, ji * TH + st * 512:
                                                   ji * TH + st * 512 + 512],
                                                start=(ji == 0),
                                                stop=(ji == nj - 1)),
                                       wait=wl,
                                       inc=s_pe if mi == 3 else None)
                            jcnt[ji] = s_pe.n
                        wmm = s_pe.n
                        for mi in range(4):
                            m = mh * 4 + mi
                            if g < ngrp - 1:
                                b.emit("dve", lambda e, m=m, st=st, mi=mi:
                                       e.tensor_add(
                                           chs(XF, m, st * 512, 512),
                                           bank(4 + mi),
                                           chs(XF, m, st * 512, 512)),
                                       wait=[(s_pe, wmm)] if mi == 0 else (),
                                       inc=s_dve)
                            else:
                                b.emit("dve", lambda e, m=m, st=st, mi=mi,
                                       ly=ly: e.scalar_tensor_tensor(
                                           chs(XF, m, st * 512, 512),
                                           bank(4 + mi),
                                           bias_col("b2_%d" % ly, m),
                                           chs(XF, m, st * 512, 512),
                                           ALU.add, ALU.add),
                                       wait=[(s_pe, wmm)] if mi == 0 else (),
                                       inc=s_dve)
                        w2_out_gate[0] = (s_dve, s_dve.n)
        for c in range(NCH):
            b.emit("dve", lambda e, c=c: e.tensor_add(
                chs(X, c), chs(X, c), chs(XF, c)), inc=s_dve)

    def stage_l1(seq, half, xn_ready):
        bvt_ld = b.dma("sp", B3[:, 0:2112].bitcast(F32), dr["l1_bvt"][:, :],
                       s_xdma, wait=[(s_dve, s_dve.n), (s_pe, s_pe.n),
                                     (s_act, s_act.n)])
        BVT1 = B3[:, 0:2112].bitcast(F32)
        wv_ld = wload(B2[:, :], "l1_wv", 0, 8, 0, 1024,
                      wait=[(s_pe, s_pe.n)])
        wn_ld = wload(WB[:, 0:256], "l1_wn", 0, 8, 0, 32,
                      wait=[(s_pe, s_pe.n)])
        ev_g = [0]
        for s in range(8):
            for k in range(8):
                for st in range(2):
                    b.emit("pe", lambda e, s=s, k=k, st=st: e.matmul(
                        bank(st), chs(XN, k)[:, s * 128:(s + 1) * 128],
                        B2[:, k * 1024 + st * 512:k * 1024 + (st + 1) * 512],
                        start=(k == 0), stop=(k == 7)),
                        wait=([(s_wdma, wv_ld), (s_dve, xn_ready)]
                              if (s == 0 and k == 0 and st == 0) else
                              ([(s_dve, ev_g[0])] if k == 0 and st == 0
                               else ())),
                        inc=None)
                b.emit("pe", lambda e, s=s, k=k: e.matmul(
                    bank(2, 32), chs(XN, k)[:, s * 128:(s + 1) * 128],
                    WB[:, k * 32:(k + 1) * 32],
                    start=(k == 0), stop=(k == 7)),
                    wait=[(s_wdma, wn_ld)] if (s == 0 and k == 0) else (),
                    inc=s_pe if k == 7 else None)
            vp = s_pe.n
            for st in range(2):
                b.emit("dve", lambda e, s=s, st=st: e.tensor_add(
                    B4[:, s * 1024 + st * 512:s * 1024 + (st + 1) * 512],
                    bank(st), BVT1[:, st * 512:(st + 1) * 512]),
                    wait=[(s_pe, vp), (s_xdma, bvt_ld)]
                    if st == 0 else (), inc=s_dve)
            b.emit("dve", lambda e, s=s: e.tensor_add(
                NTT[:, s * 16:(s + 1) * 16], bank(2, 16),
                BVT1[:, 1024:1040]), inc=s_dve)
            b.emit("dve", lambda e, s=s: e.tensor_copy(
                LRT[:, s * 8:(s + 1) * 8], bank(2, 8, 16)), inc=s_dve)
            ev_g[0] = s_dve.n
        vproj_done = s_dve.n

        lrt3 = LRT[:, :].rearrange("p (s e) -> p s e", e=8)
        b.emit("dve", lambda e: e.reduce_max(RST[:, 0:8], lrt3, axis=AX.X),
               wait=[(s_dve, vproj_done)], inc=s_dve)
        rmx = s_dve.n
        for s in range(8):
            b.emit("dve", lambda e, s=s: e.tensor_single_scalar(
                LRT[:, s * 8:(s + 1) * 8], LRT[:, s * 8:(s + 1) * 8],
                RST[:, s:s + 1], ALU.subtract),
                wait=[(s_dve, rmx)] if s == 0 else (), inc=s_dve)
        sub_r = s_dve.n
        b.emit("act", lambda e: e.activation(MU[:, 0:64], LRT[:, :], AF.Exp),
               wait=[(s_dve, sub_r)], inc=s_act)
        exp_r = s_act.n
        prb3 = MU[:, 0:64].rearrange("p (s e) -> p s e", e=8)
        b.emit("dve", lambda e: e.reduce_max(RST[:, 8:16], prb3, axis=AX.X),
               wait=[(s_act, exp_r)], inc=s_dve)
        p1d = s_dve.n
        for s in range(8):
            b.emit("dve", lambda e, s=s: e.tensor_single_scalar(
                MU[:, 64 + s * 8:64 + (s + 1) * 8], MU[:, s * 8:(s + 1) * 8],
                RST[:, 8 + s:9 + s], ALU.is_equal),
                wait=[(s_dve, p1d)] if s == 0 else (), inc=s_dve)
        b.emit("dve", lambda e: e.tensor_scalar(
            MU[:, 64:128], MU[:, 64:128], -1e30, None, ALU.mult),
            inc=s_dve)
        b.emit("dve", lambda e: e.tensor_add(MU[:, 64:128], MU[:, 64:128],
                                             MU[:, 0:64]), inc=s_dve)
        sup_d = s_dve.n
        prb4 = MU[:, 64:128].rearrange("p (s e) -> p s e", e=8)
        b.emit("dve", lambda e: e.reduce_max(RST[:, 16:24], prb4, axis=AX.X),
               wait=[(s_dve, sup_d)], inc=s_dve)
        b.emit("dve", lambda e: e.tensor_add(RST[:, 24:32], RST[:, 8:16],
                                             RST[:, 16:24]), inc=s_dve)
        sadd = s_dve.n
        for s in range(8):
            b.emit("dve", lambda e, s=s: e.reciprocal(
                RST[:, 32 + s:33 + s], RST[:, 24 + s:25 + s]),
                wait=[(s_dve, sadd)] if s == 0 else (), inc=s_dve)
        b.emit("dve", lambda e: e.tensor_mul(RST[:, 40:48], RST[:, 24:32],
                                             RST[:, 32:40]), inc=s_dve)
        m_done = s_dve.n
        cp_n = [0] * 8
        for s_ in range(8):
            slot = 2048 + (s_ % 2) * 512
            wl = [(s_dve, m_done)] if s_ == 0 else []
            if s_ >= 2:
                wl.append((s_act, cp_n[s_ - 2]))
            b.emit("pe", lambda e, s_=s_, slot=slot: e.transpose(
                PS[0:1, slot:slot + 128],
                RST[:, 40 + s_:41 + s_], IDF[:, :]),
                wait=wl, inc=s_pe)
            trm = s_pe.n
            cp_n[s_] = b.emit("act", lambda e, s_=s_, slot=slot: e.copy(
                B1[0:1, 2048 + 128 * s_:2176 + 128 * s_],
                PS[0:1, slot:slot + 128]),
                wait=[(s_pe, trm)], inc=s_act)
        mrow_a = s_act.n
        for st in range(2):
            b.emit("pe", lambda e, st=st: e.matmul(
                bank(2 + st), ONE[0:1, 0:128],
                B1[0:1, 2048 + st * 512:2048 + (st + 1) * 512],
                start=True, stop=True),
                wait=[(s_act, mrow_a)] if st == 0 else (),
                inc=s_pe if st == 1 else None)
        msb_mm = s_pe.n
        b.emit("dve", lambda e: e.tensor_copy(IV[:, :], psym(2 * 512, 1024)),
               wait=[(s_pe, msb_mm)], inc=s_dve)
        ms_done = s_dve.n

        ngrp = 4
        w2_out_gate = [(s_dve, s_dve.n)]
        for g in range(ngrp):
            jlist = list(range(g * 8, (g + 1) * 8))
            ev_prev = {}
            for ji, j in enumerate(jlist):
                sl = ji % 3
                bs = (ji % 2) * 2
                gld = wload(WA[:, sl * TH:sl * TH + 1024], "g_w1", 0, 8,
                            j * 128, 128,
                            wait=[(s_pe, s_pe.n)] if ji < 3 or g == 0 else
                            [(s_pe, ev_prev.get(jlist[ji - 3], s_pe.n)
                              if ji >= 3 else s_pe.n)])
                for k in range(NCH):
                    for st in range(2):
                        wl = []
                        if k == 0 and st == 0:
                            wl.append((s_wdma, gld))
                            if ji == 0 and g == 0:
                                wl.append((s_dve, ms_done))
                            if ji - 2 >= 0 and jlist[ji - 2] in ev_prev:
                                wl.append((s_act, ev_prev[jlist[ji - 2]]))
                        b.emit("pe", lambda e, k=k, st=st, bs=bs, sl=sl:
                               e.matmul(bank(bs + st),
                                        WA[:, sl * TH + k * 128:
                                           sl * TH + (k + 1) * 128],
                                        chs(XN, k)[:, st * 512:(st + 1) * 512],
                                        start=(k == 0), stop=(k == 7)),
                               wait=wl,
                               inc=s_pe if (k == 7 and st == 1) else None)
                gmm = s_pe.n
                ev_prev[j] = b.emit("act", lambda e, bs=bs, j=j:
                                    e.activation(B3[:, (j % 8) * TH:
                                                    (j % 8 + 1) * TH],
                                                 bank(bs, 1024),
                                                 AF.Gelu_apprx_tanh,
                                                 bias=bias_col("gb1", j)),
                                    wait=[(s_pe, gmm)], inc=s_act)
            hg_all = s_act.n
            for mh in range(2):
                for st in range(2):
                    w2e_gate = w2_out_gate[0]
                    jcnt = {}
                    for ji, j in enumerate(jlist):
                        w2ld = wload(WA[:, (ji % 3) * TH:(ji % 3) * TH + 1024],
                                     "g_w2", j * 128, 1, 0, 1024,
                                     wait=[(s_pe, jcnt[ji - 3])]
                                     if ji >= 3 else [(s_pe, s_pe.n)])
                        for mi in range(4):
                            m = mh * 4 + mi
                            wl = []
                            if mi == 0:
                                wl.append((s_wdma, w2ld))
                                wl.append((s_act, hg_all))
                                if ji == 0:
                                    wl.append(w2e_gate)
                            b.emit("pe", lambda e, ji=ji, mi=mi, m=m, st=st:
                                   e.matmul(bank(4 + mi),
                                            WA[:, (ji % 3) * TH + m * 128:
                                               (ji % 3) * TH + (m + 1) * 128],
                                            B3[:, ji * TH + st * 512:
                                               ji * TH + st * 512 + 512],
                                            start=(ji == 0), stop=(ji == 7)),
                                   wait=wl, inc=s_pe if mi == 3 else None)
                        jcnt[ji] = s_pe.n
                    wmm = s_pe.n
                    for mi in range(4):
                        m = mh * 4 + mi
                        if g == 0:
                            b.emit("dve", lambda e, m=m, st=st, mi=mi:
                                   e.tensor_copy(chs(XF, m, st * 512, 512),
                                                 bank(4 + mi)),
                                   wait=[(s_pe, wmm)] if mi == 0 else (),
                                   inc=s_dve)
                        elif g < ngrp - 1:
                            b.emit("dve", lambda e, m=m, st=st, mi=mi:
                                   e.tensor_add(chs(XF, m, st * 512, 512),
                                                bank(4 + mi),
                                                chs(XF, m, st * 512, 512)),
                                   wait=[(s_pe, wmm)] if mi == 0 else (),
                                   inc=s_dve)
                        else:
                            b.emit("dve", lambda e, m=m, st=st, mi=mi:
                                   e.scalar_tensor_tensor(
                                       chs(XF, m, st * 512, 512),
                                       bank(4 + mi), bias_col("gb2", m),
                                       chs(XF, m, st * 512, 512),
                                       ALU.add, ALU.add),
                                   wait=[(s_pe, wmm)] if mi == 0 else (),
                                   inc=s_dve)
                    w2_out_gate[0] = (s_dve, s_dve.n)
        for c in range(NCH):
            b.emit("dve", lambda e, c=c: e.tensor_mul(
                chs(XF, c), chs(XF, c), IV[:, :]), inc=s_dve)
            b.emit("dve", lambda e, c=c: e.tensor_add(
                chs(X, c), chs(X, c), chs(XF, c)), inc=s_dve)
        ffn_done = s_dve.n

        if half == 0:
            b.emit("dve", lambda e: e.memset(MSB[:, :], 0.0), inc=s_dve)
            b.emit("dve", lambda e: e.memset(MBF[:, :], 0.0), inc=s_dve)
        def qk1_evac(m, ps, pe_cnt):
            dst = chs(B2, m) if m < 8 else chs(B3, m - 8)
            a = b.emit("act", lambda e, m=m, ps=ps, dst=dst: e.activation(
                dst, ps, AF.Identity, bias=bias_col("bqk", m)),
                wait=[(s_pe, pe_cnt)], inc=s_act)
            return (s_act, a)
        proj_fmaj("l1_wqk", None, DM, 2048, lambda k: chs(XN, k), qk1_evac,
                  rhs_wait=lambda k: [(s_dve, ffn_done)] if k == 0 else [])
        qk_done = s_act.n

        prev_invnb = [s_act.n]
        ctxev = [0, 0]
        xncp = [0, 0]
        kttp = [0, 0]
        msbp = [0, 0]
        last_ct = [0]
        nrwp = [0]
        mbf_dn = [0]
        for c in range(8):
            b.emit("pe", lambda e, c=c: e.transpose(
                PS[0:16, 4 * 512:4 * 512 + 128],
                NTT[:, c * 16:(c + 1) * 16], IDF[:, :]),
                wait=[(s_act, prev_invnb[0])] +
                     ([(s_dve, nrwp[0])] if nrwp[0] else []), inc=s_pe)
            trn = s_pe.n
            b.emit("dve", lambda e: e.tensor_scalar_mul(
                NRW[0:16, 0:128], psym(4 * 512, 128)[0:16, :], -1.0),
                wait=[(s_pe, trn)], inc=s_dve)
            nrwp[0] = s_dve.n
            for st in range(2):
                b.emit("pe", lambda e, st=st: e.matmul(
                    bank(2 + st), NRW[0:16, 0:128],
                    BLK16[0:16, st * 512:(st + 1) * 512],
                    start=True, stop=True),
                    wait=[(s_dve, nrwp[0])] if st == 0 else (),
                    inc=s_pe if st == 1 else None)
            nb_mm = s_pe.n
            prev_invnb[0] = b.emit("act", lambda e: e.activation(
                B1[:, 0:TH], psym(2 * 512, 1024), AF.Exp),
                wait=[(s_pe, nb_mm)] +
                     ([(s_dve, last_ct[0])] if last_ct[0] else []), inc=s_act)
            invnb_d = prev_invnb[0]

            for hp in range(8):
                par = hp % 2
                for h64 in range(2):
                    tp = (0, 0) if h64 == 0 else (64, 0)
                    if hp == 0 and h64 == 0 and c == 0:
                        qw = [(s_act, qk_done), (s_dve, s_dve.n)]
                    elif h64 == 0:
                        qw = []
                        gate = mbf_dn[0] if hp == 0 else ctxev[par]
                        if gate:
                            qw.append((s_dve, gate))
                    else:
                        qw = []
                    b.emit("pe", lambda e, hp=hp, c=c, h64=h64, tp=tp,
                           par=par: e.matmul(
                               bank(1, 64, par * 128 + h64 * 64),
                               B2[h64 * 64:(h64 + 1) * 64,
                                  hp * TH + c * 128:hp * TH + (c + 1) * 128],
                               MBF[h64 * 64:(h64 + 1) * 64,
                                   hp * 64:(hp + 1) * 64],
                               start=True, stop=False, tile_position=tp),
                           wait=qw, inc=None)
                    b.emit("pe", lambda e, hp=hp, c=c, h64=h64, tp=tp,
                           par=par: e.matmul(
                               bank(0, 128, (par * 2 + h64) * 128),
                               B3[h64 * 64:(h64 + 1) * 64,
                                  hp * TH + c * 128:hp * TH + (c + 1) * 128],
                               B2[h64 * 64:(h64 + 1) * 64,
                                  hp * TH + c * 128:hp * TH + (c + 1) * 128],
                               start=True, stop=True, tile_position=tp),
                           inc=s_pe)
                    smm = s_pe.n
                    b.emit("dve", lambda e, h64=h64, par=par: e.tensor_mul(
                        B1[:, TH + (par * 2 + h64) * 128:
                           TH + (par * 2 + h64 + 1) * 128],
                        bank(0, 128, (par * 2 + h64) * 128), MT[:, :]),
                        wait=[(s_pe, smm)], inc=s_dve)
                    ptd = s_dve.n
                    b.emit("pe", lambda e, hp=hp, c=c, h64=h64, par=par:
                           e.matmul(bank(1, 64, par * 128 + h64 * 64),
                                    B1[:, TH + (par * 2 + h64) * 128:
                                       TH + (par * 2 + h64 + 1) * 128],
                                    B4[:, c * 1024 + (2 * hp + h64) * 64:
                                       c * 1024 + (2 * hp + h64 + 1) * 64],
                                    start=False, stop=True),
                           wait=[(s_dve, ptd)],
                           inc=s_pe if h64 == 1 else None)
                ctxmm = s_pe.n
                ct = b.emit("dve", lambda e, hp=hp, par=par: e.tensor_mul(
                    B1[:, TH + 512 + par * 128:TH + 512 + (par + 1) * 128],
                    bank(1, 128, par * 128),
                    B1[:, hp * 128:(hp + 1) * 128]),
                    wait=[(s_pe, ctxmm), (s_act, invnb_d)], inc=s_dve)
                ctxev[par] = ct
                last_ct[0] = ct
                wl = [(s_dve, ct)]
                if xncp[par]:
                    wl.append((s_act, xncp[par]))
                b.emit("pe", lambda e, par=par: e.transpose(
                    psym(5 * 512 + par * 64, 64).bitcast(BF16),
                    B1[:, TH + 512 + par * 128:TH + 512 + (par + 1) * 128],
                    IDB[:, :]),
                    wait=wl, inc=s_pe)
                tc = s_pe.n
                xncp[par] = b.emit(
                    "act", lambda e, hp=hp, c=c, par=par: e.copy(
                        XN[:, hp * TH + c * 128:hp * TH + (c + 1) * 128],
                        psym(5 * 512 + par * 64, 64).bitcast(BF16)),
                    wait=[(s_pe, tc)], inc=s_act)
                b.emit("pe", lambda e, hp=hp, c=c, par=par: e.transpose(
                    psym(6 * 512 + par * 64, 64).bitcast(BF16),
                    B3[:, hp * TH + c * 128:hp * TH + (c + 1) * 128],
                    IDB[:, :]),
                    wait=[(s_act, kttp[par])] if kttp[par] else (),
                    inc=s_pe)
                tk = s_pe.n
                ktt = b.emit("act", lambda e, par=par: e.copy(
                    B1[:, TH + 768 + par * 128:TH + 896 + par * 128],
                    psym(6 * 512 + par * 64, 64).bitcast(BF16)),
                    wait=[(s_pe, tk)], inc=s_act)
                kttp[par] = ktt
                for h64 in range(2):
                    wl = []
                    if h64 == 0:
                        wl.append((s_act, ktt))
                        if msbp[par]:
                            wl.append((s_dve, msbp[par]))
                    b.emit("pe", lambda e, hp=hp, c=c, h64=h64, par=par:
                           e.matmul(bank(7, 64, par * 128 + h64 * 64),
                                    B1[:, TH + 768 + par * 128:
                                       TH + 896 + par * 128],
                                    B4[:, c * 1024 + (2 * hp + h64) * 64:
                                       c * 1024 + (2 * hp + h64 + 1) * 64],
                                    start=True, stop=True),
                           wait=wl, inc=s_pe if h64 == 1 else None)
                dmm = s_pe.n
                b.emit("dve", lambda e, hp=hp, par=par: e.tensor_add(
                    MSB[0:64, hp * 64:(hp + 1) * 64],
                    MSB[0:64, hp * 64:(hp + 1) * 64],
                    bank(7, 64, par * 128)[0:64, :]),
                    wait=[(s_pe, dmm)], inc=s_dve)
                b.emit("dve", lambda e, hp=hp, par=par: e.tensor_add(
                    MSB[64:128, hp * 64:(hp + 1) * 64],
                    MSB[64:128, hp * 64:(hp + 1) * 64],
                    bank(7, 64, par * 128 + 64)[64:128, :]),
                    inc=s_dve)
                msbp[par] = s_dve.n
            b.emit("dve", lambda e: e.tensor_copy(MBF[:, :], MSB[:, :]),
                   inc=s_dve)
            mbf_dn[0] = s_dve.n

        ctx_ready = s_act.n

        def wo1_evac(m, ps, pe_cnt):
            d = b.emit("dve", lambda e, m=m, ps=ps: e.scalar_tensor_tensor(
                chs(X, m), ps, bias_col("bo1", m), chs(X, m),
                ALU.add, ALU.add), wait=[(s_pe, pe_cnt)], inc=s_dve)
            return (s_dve, d)
        proj_fmaj("l1_wo", None, DM, DM, lambda k: chs(XN, k), wo1_evac,
                  rhs_wait=lambda k: [(s_act, ctx_ready)] if k == 0 else [])

    for nm in dump_bufs:
        b.emit("pool", lambda e, buf=sb[nm]: e.memset(buf[:, :], 0.0))
    for dst, src in ((BIA, "bias"), (BVT0, "l0_bvt"), (MCS, "m_causal"),
                     (MBS, "m_band"), (MT, "m_tril"), (IDB, "c_id_bf"),
                     (IDF, "c_id_f32"), (ONE, "c_ones_bf"),
                     (BDG, "c_bdiag_bf"), (PRP, "c_permpair"),
                     (PRH, "c_permhalf"), (BLK16, "c_blk16")):
        b.dma("sp", dst[:, :], dr[src][:, :], s_xdma)
    bar()

    x_ready = 0
    for seq, half in units:
        x_ready = load_x(seq, half)
        bar()
        xn_done = ln_top(x_ready)
        if "l0a" in stages:
            stage_l0_attn(seq, half, xn_done)
            bar()
        if "l0f" in stages:
            stage_l0_ffn(seq, half, xn_done)
        bar()
        if "d_mid" in dbg_t and (seq, half) == (0, 1):
            b.dma("sp", dbg_t["d_mid"][:, :], X[:, :], s_out,
                  wait=[(s_dve, s_dve.n)])
            bar()
        if "l1" in stages:
            xn2_done = ln_top(x_ready)
            stage_l1(seq, half, xn2_done)
            bar()
        if "d_xn" in dbg_t:
            for c in range(NCH):
                b.emit("dve", lambda e, c=c: e.tensor_copy(chs(XF, c), chs(XN, c)),
                       inc=s_dve)
            b.dma("sp", dbg_t["d_xn"][:, :], XF[:, :], s_out,
                  wait=[(s_dve, s_dve.n)])
        if "d_x" in dbg_t:
            b.dma("sp", dbg_t["d_x"][:, :], X[:, :], s_out,
                  wait=[(s_dve, s_dve.n)])
        bar()
        store_x(seq, half, wait=[(s_dve, s_dve.n)])
        bar()

    if dump_bufs:
        s_dmp = b.sem("dmp")
        s_dmp.h = stack.enter_context(nc.semaphore("dmp"))
        for nm in dump_bufs:
            buf = sb[nm]
            dt_ = nc.dram_tensor("dump_" + nm, [128, buf.shape[1]], F32,
                                 kind="ExternalOutput").ap()
            b.dma("pool", dt_[:, :], buf[:, :], s_dmp,
                  wait=[(s_dve, s_dve.n), (s_act, s_act.n), (s_pe, s_pe.n)])
        b.emit("pool", lambda e: e.wait_ge(s_dmp.h, s_dmp.n))

    b.emit("sp", lambda e: e.wait_ge(s_out.h, s_out.n))
    b.replay()
    return nc, stack, dbg_t


def kernel(**inputs):
    H = _host_prep(inputs)
    nc, stack, _ = build_program()
    in_maps = []
    for c in range(8):
        m = dict(H)
        m["xin"] = _per_core_x(inputs, c)
        in_maps.append(m)
    res = run_bass_kernel_spmd(nc, in_maps, core_ids=list(range(8)))
    stack.close()
    out = np.empty((16, TT, DM), np.float32)
    for c in range(8):
        y = res.results[c]["yout"]
        out[2 * c] = y[0].T
        out[2 * c + 1] = y[1].T
    return out


# version 07 (final)
# speedup vs baseline: 1.0381x; 1.0381x over previous
import sys
for _p in ("/opt/trn_rl_repo",):
    if _p not in sys.path:
        sys.path.insert(0, _p)
import math
import numpy as np
import ml_dtypes
import concourse.bass as bass
import concourse.mybir as mybir
from concourse.bass_utils import run_bass_kernel_spmd

F32 = mybir.dt.float32
BF16 = mybir.dt.bfloat16
AF = mybir.ActivationFunctionType
ALU = mybir.AluOpType
AX = mybir.AxisListType

BF = ml_dtypes.bfloat16
TT = 2048
TH = 1024
DM = 1024
NCH = DM // 128
HID = 2730
HIDP = 2816
NEG = np.float32(-1e30)

ENGS = ("pe", "act", "dve", "pool", "sp")



QHEADS = [0, 4, 1, 5, 2, 6, 3, 7, 8, 12, 9, 13, 10, 14, 11, 15]


def _host_prep(I):
    H = {}
    f32 = lambda a: np.ascontiguousarray(np.asarray(a), dtype=np.float32)

    qperm = np.concatenate([64 * h + np.arange(64) for h in QHEADS])
    H["l0_wq"] = f32(np.asarray(I["L0_attn_Wq"])[:, qperm])
    H["l0_wk"] = f32(I["L0_attn_Wk"])
    H["l0_wv"] = f32(I["L0_attn_Wv"])
    H["l0_wo"] = f32(np.asarray(I["L0_attn_Wo"])[qperm, :])
    H["f_wqkv"] = f32(I["L0_ffn_wqkv"])
    H["f_wo"] = f32(I["L0_ffn_wo"])
    w1 = np.zeros((2, DM, 2 * HIDP), np.float32)
    w1[:, :, :HID] = np.asarray(I["L0_ffn_w1"])[:, :, :HID]
    w1[:, :, HIDP:HIDP + HID] = np.asarray(I["L0_ffn_w1"])[:, :, HID:]
    H["f_w1"] = w1
    w2 = np.zeros((2, HIDP, DM), np.float32)
    w2[:, :HID, :] = np.asarray(I["L0_ffn_w2"])
    H["f_w2"] = w2
    H["l1_wqk"] = f32(I["L1_attn_Wqk"])
    wvn = np.zeros((DM, 1056), np.float32)
    wvn[:, :1024] = np.asarray(I["L1_attn_Wv"])
    wvn[:, 1024:1040] = np.asarray(I["L1_attn_Wn"])
    wvn[:, 1040:1048] = np.asarray(I["L1_ffn_router_w"])
    H["l1_wvn"] = wvn
    H["l1_wo"] = f32(I["L1_attn_Wo"])
    H["g_w1"] = f32(I["L1_ffn_w1"])
    H["g_w2"] = f32(I["L1_ffn_w2"])

    def packt(w, cw):
        K, C = w.shape
        kch, nt = K // 128, C // cw
        return np.ascontiguousarray(
            w.reshape(kch, 128, nt, cw).transpose(1, 2, 0, 3)
             .reshape(128, nt * kch * cw))

    def packr(w, kch, mch):
        K, C = w.shape
        c = C // mch
        return np.ascontiguousarray(
            w.reshape(kch, 128, mch, c).transpose(1, 0, 2, 3)
             .reshape(128, kch * mch * c))

    H["l0_wq"] = packt(H["l0_wq"], 128)
    H["l0_wk"] = packt(H["l0_wk"], 128)
    H["l0_wv"] = packt(H["l0_wv"], 256)
    H["l0_wo"] = packr(H["l0_wo"], 8, 8)
    H["f_wv"] = np.stack([packt(H["f_wqkv"][l][:, 1024:1536], 512)
                          for l in range(2)])
    H["f_wqkv"] = np.stack([packt(H["f_wqkv"][l][:, 0:1024], 128)
                            for l in range(2)])
    H["f_w1"] = np.stack([packt(H["f_w1"][l], 128) for l in range(2)])
    H["f_wo"] = np.stack([packr(H["f_wo"][l], 4, 8) for l in range(2)])
    H["l1_wv"] = packt(H["l1_wvn"][:, 0:1024], 1024)
    H["l1_wn"] = packt(H["l1_wvn"][:, 1024:1056], 32)
    del H["l1_wvn"]
    H["l1_wqk"] = packt(H["l1_wqk"], 128)
    H["l1_wo"] = packt(H["l1_wo"], 128)
    H["g_w1"] = packt(H["g_w1"], 128)

    def cols(v, n):
        v = np.asarray(v, np.float32)
        assert v.size == 128 * n, (v.size, n)
        return v.reshape(n, 128).T

    parts = []
    for name, n in _BIAS_LAYOUT:
        if name == "bq":
            v = np.asarray(I["L0_attn_bq"])[qperm]
        elif name == "bk":
            v = I["L0_attn_bk"]
        elif name == "bo0":
            v = I["L0_attn_bo"]
        elif name in ("qs0", "ks0", "qs1", "ks1"):
            src = I["L0_ffn_q_scale" if name[0] == "q" else "L0_ffn_k_scale"]
            v = np.tile(np.asarray(src)[int(name[2])], 2)
        elif name in ("g0", "g1"):
            v = np.asarray(I["L0_ffn_ln_g"])[int(name[1])]
        elif name in ("lb0", "lb1"):
            v = np.asarray(I["L0_ffn_ln_b"])[int(name[2])]
        elif name in ("b1_0", "b1_1"):
            b1p = np.zeros(2 * HIDP, np.float32)
            b1 = np.asarray(I["L0_ffn_b1"])[int(name[3])]
            b1p[:HID] = b1[:HID]
            b1p[HIDP:HIDP + HID] = b1[HID:]
            v = b1p
        elif name in ("b2_0", "b2_1"):
            v = np.asarray(I["L0_ffn_b2"])[int(name[3])]
        elif name == "bqk":
            v = I["L1_attn_bqk"]
        elif name == "bo1":
            v = I["L1_attn_bo"]
        elif name == "gb1":
            v = I["L1_ffn_b1"]
        elif name == "gb2":
            v = I["L1_ffn_b2"]
        parts.append(cols(v, n))
    H["bias"] = np.ascontiguousarray(np.concatenate(parts, axis=1))
    assert H["bias"].shape[1] == NBIAS

    bvt = np.zeros((128, 1056), np.float32)
    bvt[:, :1024] = np.asarray(I["L1_attn_bv"])[None, :]
    bvt[:, 1024:1040] = np.asarray(I["L1_attn_bn"])[None, :]
    H["l1_bvt"] = bvt
    H["l0_bvt"] = f32(np.broadcast_to(np.asarray(I["L0_attn_bv"])[None, :],
                                      (128, 256)))

    pos = np.arange(TT, dtype=np.float64)
    fr = np.exp(-(np.arange(32, dtype=np.float64) / 32) * math.log(10000.0))
    ang = np.repeat(fr, 2)[:, None] * pos[None, :]
    dup = lambda a: np.ascontiguousarray(
        np.concatenate([a, a], 0).astype(BF))
    cos64, sin64 = np.cos(ang), np.sin(ang)
    H["t_l0a"] = np.stack([dup(cos64 * 0.125), dup(sin64 * 0.125),
                           dup(cos64), dup(sin64)])
    fr2 = 1.0 / 10000.0 ** (np.arange(32, dtype=np.float64) / 32)
    fq = fr2[:, None] * pos[None, :]
    base = (np.arange(0, 64, 2, dtype=np.float64) + 25.6) / 89.6
    power = (pos - TT // 2) / 256.0
    sc = base[:, None] ** power[None, :]
    c2, s2_ = np.cos(fq), np.sin(fq)
    mk = lambda a: dup(np.concatenate([a, a], 0))
    H["t_ffn"] = np.stack([mk(c2 * sc), mk(s2_ * sc),
                           mk(c2 / sc), mk(s2_ / sc)])

    ii = np.arange(128)[:, None]; jj = np.arange(128)[None, :]
    causal = np.where(jj <= ii, 0.0, NEG)
    band = np.where(ii <= jj + 0, 0.0, NEG)
    H["m_causal"] = np.ascontiguousarray(causal.T.astype(BF))
    H["m_band"] = np.ascontiguousarray(band.T.astype(BF))
    H["m_tril"] = np.where(jj >= ii, 1.0, 0.0).astype(BF)
    H["c_id_bf"] = np.eye(128, dtype=BF)
    H["c_id_f32"] = np.eye(128, dtype=np.float32)
    H["c_ones_bf"] = np.ones((128, 128), dtype=BF)
    bd = np.zeros((128, 128), np.float32)
    bd[:64, :64] = 1.0; bd[64:, 64:] = 1.0
    H["c_bdiag_bf"] = bd.astype(BF)
    e16 = np.zeros((128, 1024), np.float32)
    for _k in range(16):
        e16[_k, _k * 64:(_k + 1) * 64] = 1.0
    H["c_blk16"] = e16.astype(BF)
    pp = np.zeros((128, 128), np.float32)
    for i in range(64):
        pp[2 * i + 1, 2 * i] = -1.0
        pp[2 * i, 2 * i + 1] = 1.0
    H["c_permpair"] = pp.astype(BF)
    ph = np.zeros((128, 128), np.float32)
    for b in range(2):
        for i in range(32):
            ph[b * 64 + i + 32, b * 64 + i] = -1.0
            ph[b * 64 + i, b * 64 + i + 32] = 1.0
    H["c_permhalf"] = ph.astype(BF)
    return H


_BIAS_LAYOUT = (("bq", 8), ("bk", 2), ("bo0", 8), ("qs0", 1), ("ks0", 1),
                ("qs1", 1), ("ks1", 1), ("g0", 8), ("lb0", 8), ("g1", 8),
                ("lb1", 8), ("b1_0", 44), ("b1_1", 44), ("b2_0", 8),
                ("b2_1", 8), ("bqk", 16), ("bo1", 8), ("gb1", 32), ("gb2", 8))
NBIAS = sum(n for _, n in _BIAS_LAYOUT)
BIDX = {}
_off = 0
for _nm, _n in _BIAS_LAYOUT:
    BIDX[_nm] = _off
    _off += _n


def _per_core_x(I, c):
    x = np.asarray(I["L0_attn_x"])
    return np.ascontiguousarray(
        np.stack([x[2 * c].T, x[2 * c + 1].T]).astype(np.float32))



class Sem:
    __slots__ = ("name", "h", "n")
    def __init__(self, name):
        self.name, self.h, self.n = name, None, 0


class Bld:
    def __init__(self, nc):
        self.nc = nc
        self.q = {e: [] for e in ENGS}
        self.sems = []
        self.bar = self.sem("bar")
        self.nbar = 0
        self.dirty = {e: False for e in ENGS}

    def sem(self, name):
        s = Sem(name)
        self.sems.append(s)
        return s

    def emit(self, eng, fn, wait=(), inc=None, inc_n=1):
        w = tuple((s, v) for (s, v) in wait if v > 0)
        if inc is not None:
            inc.n += inc_n
            self.dirty[eng] = False
        else:
            self.dirty[eng] = True
        tgt = (inc, inc_n) if inc is not None else None

        def thunk(e, fn=fn, w=w, tgt=tgt):
            for s, v in w:
                e.wait_ge(s.h, v)
            ins = fn(e)
            if tgt is not None:
                if ins is None:
                    ins = e.nop()
                ins.then_inc(tgt[0].h, tgt[1])
        self.q[eng].append(thunk)
        return inc.n if inc is not None else None

    def dma(self, eng, out_ap, in_ap, inc, wait=()):
        n = self.emit(eng, lambda e, o=out_ap, i=in_ap: e.dma_start(out=o, in_=i),
                      wait=wait, inc=inc, inc_n=16)
        self.dirty[eng] = False
        return n

    def barrier(self, dma_sems=(), eng_sems=()):
        self.nbar += 1
        tgt = 5 * self.nbar
        self.bar.n = tgt
        drains = tuple((s, s.n) for s in dma_sems) + \
            tuple((s, s.n) for s in eng_sems)
        for e in ENGS:
            def thunk(en, s=self.bar, t=tgt, dr=drains):
                for ds, dv in dr:
                    if dv > 0:
                        en.wait_ge(ds.h, dv)
                en.nop().then_inc(s.h, 1)
                en.wait_ge(s.h, t)
            self.q[e].append(thunk)

    def finalize(self, stack):
        for s in self.sems:
            s.h = stack.enter_context(self.nc.semaphore(s.name))

    def replay(self):
        nc = self.nc
        with nc.Block() as block:
            @block.tensor
            def _(e):
                for t in self.q["pe"]:
                    t(e)

            @block.scalar
            def _(e):
                for t in self.q["act"]:
                    t(e)

            @block.vector
            def _(e):
                for t in self.q["dve"]:
                    t(e)

            @block.gpsimd
            def _(e):
                for t in self.q["pool"]:
                    t(e)

            @block.sync
            def _(e):
                for t in self.q["sp"]:
                    t(e)



def build_program(debug=(), units=((0, 0), (0, 1), (1, 0), (1, 1)),
                  stages=("l0a", "l0f", "l1"), dump_bufs=()):
    import contextlib
    stack = contextlib.ExitStack()
    nc = bass.Bass("TRN2", target_bir_lowering=False,
                   detect_race_conditions=False)

    dr = {}
    def din(name, shape, dt=F32):
        dr[name] = nc.dram_tensor(name, list(shape), dt, kind="ExternalInput").ap()

    din("xin", (2, DM, TT))
    din("l0_wq", (128, 8192)); din("l0_wk", (128, 2048))
    din("l0_wv", (128, 2048)); din("l0_wo", (128, 8192))
    din("f_wqkv", (2, 128, 8192)); din("f_wv", (2, 128, 4096))
    din("f_wo", (2, 128, 4096))
    din("f_w1", (2, 128, 16 * HIDP)); din("f_w2", (2, HIDP, DM))
    din("l1_wqk", (128, 16384)); din("l1_wv", (128, 8192))
    din("l1_wn", (128, 256)); din("l1_wo", (128, 8192))
    din("g_w1", (128, 32768)); din("g_w2", (4096, DM))
    din("bias", (128, NBIAS))
    din("l1_bvt", (128, 1056)); din("l0_bvt", (128, 256))
    din("t_l0a", (4, 128, TT), BF16); din("t_ffn", (4, 128, TT), BF16)
    din("m_causal", (128, 128), BF16); din("m_band", (128, 128), BF16)
    din("m_tril", (128, 128), BF16)
    din("c_id_bf", (128, 128), BF16); din("c_id_f32", (128, 128))
    din("c_ones_bf", (128, 128), BF16); din("c_bdiag_bf", (128, 128), BF16)
    din("c_permpair", (128, 128), BF16); din("c_permhalf", (128, 128), BF16)
    din("c_blk16", (128, 1024), BF16)
    yout = nc.dram_tensor("yout", [2, DM, TT], F32, kind="ExternalOutput").ap()
    dbg_t = {}
    for nm in debug:
        dbg_t[nm] = nc.dram_tensor(nm, [128, NCH * TH], F32,
                                   kind="ExternalOutput").ap()

    sb = {}
    def sbuf(name, cols, dt=BF16):
        sb[name] = stack.enter_context(nc.sbuf_tensor(name, [128, cols], dt))
        return sb[name]

    X    = sbuf("X", NCH * TH, F32)
    XF   = sbuf("XF", NCH * TH, F32)
    XN   = sbuf("XN", NCH * TH, BF16)
    B1   = sbuf("B1", NCH * TH, BF16)
    B2   = sbuf("B2", NCH * TH, BF16)
    B3   = sbuf("B3", NCH * TH, BF16)
    B4   = sbuf("B4", NCH * TH, BF16)
    KD   = sbuf("KD", 2 * TT, BF16)
    VT0  = sbuf("VT0", 16 * 256, BF16)
    KT   = sbuf("KT", 2 * 2048, BF16)
    VTT  = sbuf("VTT", 2 * 2048, BF16)
    WA   = sbuf("WA", 3 * TH, BF16)
    WB   = sbuf("WB", 2 * TH, BF16)
    TBL  = sbuf("TBL", 2 * TH, BF16)
    MU   = sbuf("MU", TH, F32)
    IV   = sbuf("IV", TH, F32)
    MSB  = sbuf("MSB", 512, F32)
    MBF  = sbuf("MBF", 512, BF16)
    SM   = sbuf("SM", 16, F32)
    BIA  = sbuf("BIA", NBIAS, F32)
    BVT0 = sbuf("BVT0", 256, F32)
    MCS  = sbuf("MCS", 128, BF16)
    MBS  = sbuf("MBS", 128, BF16)
    MT   = sbuf("MT", 128, BF16)
    IDB  = sbuf("IDB", 128, BF16)
    IDF  = sbuf("IDF", 128, F32)
    ONE  = sbuf("ONE", 128, BF16)
    BDG  = sbuf("BDG", 128, BF16)
    PRP  = sbuf("PRP", 128, BF16)
    PRH  = sbuf("PRH", 128, BF16)
    BLK16 = sbuf("BLK16", 1024, BF16)
    LRT  = TBL[:, 0:128].bitcast(F32)
    SHC  = sbuf("SHC", 16, F32)
    NTT  = TBL[:, 128:384].bitcast(F32)
    NRW  = TBL[:, 384:512]
    RST = MU[:, 128:176]
    SQ = B1[:, 4 * TH:6 * TH]

    PS = stack.enter_context(nc.psum_tensor("PS", [128, 4096], F32))
    def bank(i, cols=512, off=0):
        return PS[:, 512 * i + off: 512 * i + off + cols]
    def psym(col0, cols):
        return PS[:, col0:col0 + cols]

    b = Bld(nc)
    s_wdma = b.sem("wdma")
    s_xdma = b.sem("xdma")
    s_out = b.sem("odma")
    s_pe = b.sem("pe")
    s_act = b.sem("act")
    s_dve = b.sem("dve")
    b.finalize(stack)

    def bias_col(name, i=0):
        return BIA[:, BIDX[name] + i: BIDX[name] + i + 1]

    def bar():
        b.barrier((s_xdma, s_wdma, s_out), (s_pe, s_act, s_dve))

    def chs(buf, c, t0=0, n=TH, w=TH):
        return buf[:, c * w + t0: c * w + t0 + n]

    def load_x(seq, half):
        src = dr["xin"][seq, :, half * TH:(half + 1) * TH]
        return b.dma("sp", X[:, :].rearrange("p (c t) -> p c t", c=NCH),
                     src.rearrange("(c p) t -> p c t", p=128), s_xdma)

    def store_x(seq, half, wait):
        dst = yout[seq, :, half * TH:(half + 1) * TH]
        return b.dma("sp", dst.rearrange("(c p) t -> p c t", p=128),
                     X[:, :].rearrange("p (c t) -> p c t", c=NCH),
                     s_out, wait=wait)

    def load_tables(name, t0, t1, half):
        c0 = half * TH
        w = [(s_dve, s_dve.n)]
        b.dma("sp", TBL[:, 0:TH], dr[name][t0, :, c0:c0 + TH], s_xdma, wait=w)
        n = b.dma("sp", TBL[:, TH:2 * TH], dr[name][t1, :, c0:c0 + TH],
                  s_xdma, wait=list(w))
        return n

    PACKT = {"l0_wq": 128, "l0_wk": 128, "l0_wv": 256, "f_wqkv": 128,
             "f_wv": 512, "f_w1": 128, "g_w1": 128, "l1_wqk": 128,
             "l1_wo": 128, "l1_wv": 1024, "l1_wn": 32}

    def wload(dst_ap, w_name, r0, rk, c0, ck, lay=None, wait=()):
        w = dr[w_name] if lay is None else dr[w_name][lay]
        if w_name in PACKT:
            cwp = PACKT[w_name]
            assert ck == cwp and r0 == 0 and c0 % cwp == 0, (w_name, c0, ck)
            t = c0 // cwp
            return b.dma("pool", dst_ap,
                         w[:, t * rk * cwp:(t + 1) * rk * cwp],
                         s_wdma, wait=wait)
        src = w[r0:r0 + rk * 128, c0:c0 + ck]
        src = src.rearrange("(k p) m -> p k m", p=128)
        dst = dst_ap.rearrange("p (k m) -> p k m", k=rk)
        return b.dma("pool", dst, src, s_wdma, wait=wait)

    def ln_stats(src_chunks, nch, scale, banks=(0, 1, 2, 3), src_wait=None,
                 shift_info=None):
        sq_done = [0] * nch
        pe_sq = [0] * nch
        swl = src_wait if src_wait is not None else (lambda c: [])
        for c in range(nch):
            src = src_chunks(c)
            w = list(swl(c)) + ([(s_pe, pe_sq[c - 2])] if c >= 2 else [])
            sq_done[c] = b.emit("act", lambda e, c=c, src=src: e.activation(
                SQ[:, (c % 2) * TH:(c % 2) * TH + TH], src, AF.Square),
                wait=w, inc=s_act)
            for st in range(2):
                b.emit("pe", lambda e, c=c, st=st, src=src: e.matmul(
                    bank(banks[st]), ONE[:, :], src[:, st * 512:(st + 1) * 512],
                    start=(c == 0), stop=(c == nch - 1)),
                    wait=swl(c) if st == 0 else ())
            for st in range(2):
                b.emit("pe", lambda e, c=c, st=st: e.matmul(
                    bank(banks[2 + st]), ONE[:, :],
                    SQ[:, (c % 2) * TH + st * 512:(c % 2) * TH + (st + 1) * 512],
                    start=(c == 0), stop=(c == nch - 1)),
                    wait=[(s_act, sq_done[c])] if st == 0 else (),
                    inc=s_pe if st == 1 else None)
            pe_sq[c] = s_pe.n
        pe_done = s_pe.n
        b.emit("dve", lambda e: e.tensor_scalar_mul(
            MU[:, :], bank(banks[0], 1024), scale), wait=[(s_pe, pe_done)],
            inc=s_dve)
        b.emit("dve", lambda e: e.tensor_mul(IV[:, :], MU[:, :], MU[:, :]),
               inc=s_dve)
        b.emit("dve", lambda e: e.scalar_tensor_tensor(
            IV[:, :], bank(banks[2], 1024), scale, IV[:, :], ALU.mult,
            ALU.subtract), inc=s_dve)
        b.emit("dve", lambda e: e.tensor_single_scalar(
            IV[:, :], IV[:, :], 1e-5, ALU.add), inc=s_dve)
        dve_stats = s_dve.n
        b.emit("act", lambda e: e.activation(IV[:, :], IV[:, :], AF.Ln),
               wait=[(s_dve, dve_stats)], inc=s_act)
        act_stats = b.emit("act", lambda e: e.activation(
            IV[:, :], IV[:, :], AF.Exp, scale=-0.5), inc=s_act)
        return dve_stats, act_stats

    def ln_top(x_ready):
        cast_done = [0] * NCH
        for c in range(NCH):
            cast_done[c] = b.emit("dve", lambda e, c=c: e.tensor_copy(
                chs(XN, c), chs(X, c)),
                wait=[(s_xdma, x_ready)] if c == 0 else (), inc=s_dve)
        _, act_stats = ln_stats(lambda c: chs(XN, c), NCH, 1.0 / DM,
                                src_wait=lambda c: [(s_dve, cast_done[c])])
        for c in range(NCH):
            b.emit("dve", lambda e, c=c: e.tensor_sub(
                chs(XN, c), chs(XN, c), MU[:, :]),
                wait=[(s_act, act_stats)] if c == 0 else (), inc=s_dve)
            b.emit("dve", lambda e, c=c: e.tensor_mul(
                chs(XN, c), chs(XN, c), IV[:, :]), inc=s_dve)
        return s_dve.n

    def proj_fmaj(wname, lay, din_, dout, rhs_fn, evac, Tcols=TH,
                  wslots=None, mt0=0, mtn=None, rhs_wait=None, c0=0, cw=128):
        kch = din_ // 128
        mtn = mtn if mtn is not None else dout // 128
        slots = wslots or [0, 1, 2]
        ns = len(slots)
        nstr = max(1, Tcols // 512)
        rw = rhs_wait if rhs_wait is not None else (lambda k: [])
        wld = {}
        mm_done = {}
        ev_gate = {}
        mlist = list(range(mt0, mt0 + mtn))
        for i, m in enumerate(mlist[:ns]):
            wld[m] = wload(WA[:, slots[i] * TH:slots[i] * TH + kch * 128],
                           wname, 0, kch, c0 + m * cw, cw, lay=lay,
                           wait=[(s_pe, s_pe.n)] if i == 0 else ())
        for mi, m in enumerate(mlist):
            sl = slots[mi % ns]
            bs = (mi % 2) * 2
            for k in range(kch):
                lhs = WA[:, sl * TH + k * 128: sl * TH + (k + 1) * 128]
                for st in range(nstr):
                    wl = []
                    if k == 0 and st == 0:
                        wl.append((s_wdma, wld[m]))
                        if mi - 2 >= 0 and mlist[mi - 2] in ev_gate:
                            wl.append(ev_gate[mlist[mi - 2]])
                    wl += rw(k)
                    b.emit("pe", lambda e, lhs=lhs, k=k, st=st, bs=bs,
                           rhs_fn=rhs_fn: e.matmul(
                        bank(bs + st, min(512, Tcols)), lhs,
                        rhs_fn(k)[:, st * 512:st * 512 + min(512, Tcols)],
                        start=(k == 0), stop=(k == kch - 1)),
                        wait=wl,
                        inc=s_pe if (k == kch - 1 and st == nstr - 1) else None)
            mm_done[m] = s_pe.n
            ev_gate[m] = evac(m, bank(bs, Tcols), mm_done[m])
            if mi + ns < mtn:
                m2 = mlist[mi + ns]
                wld[m2] = wload(WA[:, sl * TH:sl * TH + kch * 128], wname,
                                0, kch, c0 + m2 * cw, cw, lay=lay,
                                wait=[(s_pe, mm_done[m])])
        return s_pe.n

    def rope(src_ap, dst_ap, perm, cos_ap, sin_ap, rot_bank, scr_ap,
             src_ready, label=""):
        for st in range(2):
            b.emit("pe", lambda e, st=st: e.matmul(
                bank(rot_bank + st, 512), perm[:, :],
                src_ap[:, st * 512:(st + 1) * 512], start=True, stop=True),
                wait=list(src_ready) if st == 0 else (),
                inc=s_pe if st == 1 else None)
        mmc = s_pe.n
        a = b.emit("act", lambda e: e.copy(scr_ap, bank(rot_bank, 1024)),
                   wait=[(s_pe, mmc)], inc=s_act)
        d1 = b.emit("dve", lambda e: e.tensor_mul(dst_ap, scr_ap, sin_ap),
                    wait=[(s_act, a)], inc=s_dve)
        b.emit("dve", lambda e: e.tensor_mul(scr_ap, src_ap, cos_ap),
               inc=s_dve)
        d2 = b.emit("dve", lambda e: e.tensor_add(dst_ap, dst_ap, scr_ap),
                    inc=s_dve)
        return mmc, d2

    def attn_softmax(qt, hp_list, q_ap_fn, k_ap_fn, tiles, v_ap_fn,
                     ctx_out_fn, pt_sb, p_sb, extra_score_wait=()):
        nt = len(tiles)
        prev_exp_all = [s_act.n]
        prev_ctx_read = [s_dve.n]
        for hp in hp_list:
            for h64 in range(2):
                b0 = 0 if h64 == 0 else 4
                tp = (0, 0) if h64 == 0 else (64, 0)
                for ti, (stile, msk) in enumerate(tiles):
                    bk = b0 + ti // 4
                    off = (ti % 4) * 128
                    wl = []
                    if ti == 0 and h64 == 0:
                        wl += list(extra_score_wait)
                        wl.append((s_dve, prev_ctx_read[0]))
                    if msk is not None:
                        mm = MCS if msk == 'causal' else MBS
                        b.emit("pe", lambda e, bk=bk, off=off, mm=mm:
                               e.matmul(bank(bk, 128, off), mm[:, :],
                                        IDB[:, :], start=True, stop=False),
                               wait=wl)
                        wl = []
                    b.emit("pe", lambda e, bk=bk, off=off, hp=hp, h64=h64,
                           stile=stile, tp=tp, acc=(msk is not None):
                           e.matmul(bank(bk, 128, off), q_ap_fn(hp, h64),
                                    k_ap_fn(hp, stile, h64),
                                    start=(not acc), stop=True,
                                    tile_position=tp),
                           wait=wl,
                           inc=s_pe if (ti == nt - 1) else None)
            pe_scores = s_pe.n
            ncols = nt * 128
            npass = (ncols + 1023) // 1024
            for h64 in range(2):
                b0 = 0 if h64 == 0 else 4
                b.emit("dve", lambda e, b0=b0, ncols=ncols, h64=h64:
                       e.reduce_max(SM[:, h64:h64 + 1],
                                    psym(b0 * 512, ncols), axis=AX.X),
                       wait=[(s_pe, pe_scores)] if h64 == 0 else (),
                       inc=s_dve)
                mx_done = s_dve.n
                b.emit("act", lambda e, h64=h64: e.mul(
                    SM[:, 2 + h64:3 + h64], SM[:, h64:h64 + 1], -1.0),
                    wait=[(s_dve, mx_done)], inc=s_act)
                ng = s_act.n
                for pz in range(npass):
                    cc = pz * 1024
                    cn = min(ncols - cc, 1024)
                    b.emit(
                        "act", lambda e, b0=b0, cc=cc, cn=cn, h64=h64, pz=pz:
                        e.activation(
                            p_sb[:, h64 * 2048 + pz * 1024:
                                 h64 * 2048 + pz * 1024 + cn],
                            psym(b0 * 512 + cc, cn), AF.Exp,
                            bias=SM[:, 2 + h64:3 + h64],
                            accum_out=SM[:, 4 + 2 * h64 + pz:
                                         5 + 2 * h64 + pz]),
                        wait=[(s_act, ng)], inc=s_act)
            b.emit("act", lambda e: e.copy(SM[:, 12:13], SM[:, 4:5]),
                   inc=s_act)
            exp_all = s_act.n
            sub_all = s_dve.n
            prev_exp_all[0] = exp_all
            for h64 in range(2):
                if npass == 2:
                    b.emit("dve", lambda e, h64=h64: e.tensor_add(
                        SM[:, 8 + h64:9 + h64],
                        SM[:, 4 + 2 * h64:5 + 2 * h64],
                        SM[:, 5 + 2 * h64:6 + 2 * h64]),
                        wait=[(s_act, exp_all)] if h64 == 0 else (),
                        inc=s_dve)
                else:
                    b.emit("dve", lambda e, h64=h64: e.tensor_copy(
                        SM[:, 8 + h64:9 + h64],
                        SM[:, 4 + 2 * h64:5 + 2 * h64]),
                        wait=[(s_act, exp_all)] if h64 == 0 else (),
                        inc=s_dve)
            lsum_d = s_dve.n
            b.emit("dve", lambda e: e.reciprocal(SM[:, 10:11], SM[:, 8:9]),
                   wait=[(s_dve, lsum_d)], inc=s_dve)
            b.emit("dve", lambda e: e.reciprocal(SM[:, 11:12], SM[:, 9:10]),
                   inc=s_dve)
            invl_done = s_dve.n
            tr_slot = {}
            tr_recent = []
            for h64 in range(2):
                cp = [0] * nt
                for ti in range(nt):
                    si = (h64 * nt + ti) % 8
                    tb_ap = psym((si % 2) * 512 + (si // 2) * 64,
                                 64).bitcast(BF16)
                    wl = []
                    if h64 == 0 and ti == 0:
                        wl += [(s_act, exp_all), (s_dve, sub_all)]
                    if len(tr_recent) >= 2:
                        wl.append((s_dve, tr_recent[-2]))
                    if si in tr_slot:
                        wl.append((s_dve, tr_slot[si]))
                    b.emit("pe", lambda e, h64=h64, ti=ti, tb_ap=tb_ap:
                           e.transpose(tb_ap,
                                       p_sb[:, h64 * 2048 + ti * 128:
                                            h64 * 2048 + (ti + 1) * 128],
                                       IDB[:, :]),
                           wait=wl, inc=s_pe)
                    trc = s_pe.n
                    cp[ti] = b.emit("dve", lambda e, h64=h64, ti=ti,
                                    tb_ap=tb_ap, nt=nt: e.tensor_copy(
                                        pt_sb[:, (h64 * nt + ti) * 128:
                                              (h64 * nt + ti + 1) * 128],
                                        tb_ap),
                                    wait=[(s_pe, trc)], inc=s_dve)
                    tr_slot[si] = cp[ti]
                    tr_recent.append(cp[ti])
                for ti in range(nt):
                    st_tile = tiles[ti][0]
                    pv_wl = [(s_dve, cp[ti])]
                    if h64 == 0 and ti == 0:
                        pv_wl.append((s_dve, prev_ctx_read[0]))
                    b.emit("pe", lambda e, h64=h64, ti=ti, hp=hp, nt=nt,
                           st_tile=st_tile:
                           e.matmul(bank(2, 64, h64 * 64),
                                    pt_sb[:, (h64 * nt + ti) * 128:
                                          (h64 * nt + ti + 1) * 128],
                                    v_ap_fn(hp, st_tile, h64),
                                    start=(ti == 0), stop=(ti == nt - 1)),
                           wait=pv_wl,
                           inc=s_pe if (h64 == 1 and ti == nt - 1) else None)
            ctxmm = s_pe.n
            dst, post = ctx_out_fn(hp)
            for h64 in range(2):
                b.emit("dve", lambda e, h64=h64, dst=dst:
                       e.tensor_single_scalar(
                           dst[:, h64 * 64:(h64 + 1) * 64],
                           bank(2, 64, h64 * 64),
                           SM[:, 10 + h64:11 + h64], ALU.mult),
                       wait=[(s_pe, ctxmm), (s_dve, invl_done)], inc=s_dve)
            prev_ctx_read[0] = s_dve.n
            if post is not None:
                post()

    def stage_l0_attn(seq, half, xn_ready):
        c0 = half * TH
        wo_ld = b.dma("pool", B3[:, :], dr["l0_wo"][:, :], s_wdma)
        tb = load_tables("t_l0a", 2, 3, half)
        def k_evac(m, ps, pe_cnt):
            a = b.emit("act", lambda e, m=m, ps=ps: e.activation(
                B1[:, 0:TH], ps, AF.Identity, bias=bias_col("bk", m)),
                wait=[(s_pe, pe_cnt), (s_xdma, tb), (s_dve, s_dve.n)],
                inc=s_act)
            rope(B1[:, 0:TH], KD[:, m * TT + c0:m * TT + c0 + TH], PRP,
                 TBL[:, 0:TH], TBL[:, TH:2 * TH], 4,
                 B1[:, TH:2 * TH], [(s_act, a)])
            return (s_act, a)
        proj_fmaj("l0_wk", None, DM, 256, lambda k: chs(XN, k), k_evac,
                  rhs_wait=lambda k: [(s_dve, xn_ready)] if k == 0 else [])
        wv_ld = wload(WB[:, 0:2048], "l0_wv", 0, 8, 0, 256,
                      wait=[(s_pe, s_pe.n)])
        for s in range(8):
            for k in range(8):
                b.emit("pe", lambda e, s=s, k=k: e.matmul(
                    bank(6, 256), chs(XN, k)[:, s * 128:(s + 1) * 128],
                    WB[:, k * 256:(k + 1) * 256],
                    start=(k == 0), stop=(k == 7)),
                    wait=([(s_wdma, wv_ld), (s_dve, xn_ready)]
                          if (s == 0 and k == 0) else
                          ([(s_dve, s_dve.n)] if k == 0 else ())),
                    inc=s_pe if k == 7 else None)
            vp = s_pe.n
            b.emit("dve", lambda e, s=s: e.tensor_add(
                VT0[:, (half * 8 + s) * 256:(half * 8 + s + 1) * 256],
                bank(6, 256), BVT0[:, :]), wait=[(s_pe, vp)], inc=s_dve)
        tq = load_tables("t_l0a", 0, 1, half)
        def q_evac(m, ps, pe_cnt):
            a = b.emit("act", lambda e, m=m, ps=ps: e.activation(
                B1[:, 0:TH], ps, AF.Identity, bias=bias_col("bq", m)),
                wait=[(s_pe, pe_cnt), (s_xdma, tq), (s_dve, s_dve.n)],
                inc=s_act)
            rope(B1[:, 0:TH], chs(B2, m), PRP,
                 TBL[:, 0:TH], TBL[:, TH:2 * TH], 4,
                 B1[:, TH:2 * TH], [(s_act, a)])
            return (s_act, a)
        proj_fmaj("l0_wq", None, DM, DM, lambda k: chs(XN, k), q_evac)
        prep_done_dve = s_dve.n
        prep_done_pe = s_pe.n

        def k_ap(hp, stile, h64):
            kc = 0 if hp < 4 else 1
            return KD[h64 * 64:(h64 + 1) * 64,
                      kc * TT + stile * 128: kc * TT + (stile + 1) * 128]

        def v_ap(hp, stile, h64):
            g = QHEADS[2 * hp + h64] // 4
            return VT0[:, stile * 256 + g * 64: stile * 256 + (g + 1) * 64]

        prev_wo_evac = [s_dve.n]
        prev_ctxf_evac = [s_act.n]
        import os
        qtr = range(int(os.environ.get('QT0', 0)), int(os.environ.get('QT1', 8)))
        for qt in qtr:
            qg = half * 8 + qt
            tiles = [(s, 'causal' if s == qg else None) for s in range(qg + 1)]

            def q_ap(hp, h64, qt=qt):
                return B2[h64 * 64:(h64 + 1) * 64,
                          hp * TH + qt * 128: hp * TH + (qt + 1) * 128]

            def ctx_out(hp, qt=qt):
                dst = B1[:, TH + (hp % 2) * 128: TH + (hp % 2 + 1) * 128]
                def post(hp=hp, dst=dst):
                    td = s_dve.n
                    sl = 3 * 512 + (hp % 2) * 64
                    wl = [(s_dve, td)]
                    if hp >= 2:
                        wl.append((s_act, prev_ctxf_evac[0]))
                    b.emit("pe", lambda e, dst=dst, sl=sl: e.transpose(
                        psym(sl, 64).bitcast(BF16), dst, IDB[:, :]),
                        wait=wl, inc=s_pe)
                    tc = s_pe.n
                    prev_ctxf_evac[0] = b.emit(
                        "act", lambda e, hp=hp, sl=sl: e.copy(
                            B1[:, 2 * TH + hp * 128: 2 * TH + (hp + 1) * 128],
                            psym(sl, 64).bitcast(BF16)),
                        wait=[(s_pe, tc)], inc=s_act)
                return dst, post

            attn_softmax(qt, list(range(8)), q_ap, k_ap, tiles, v_ap,
                         ctx_out, pt_sb=B1[:, 3 * TH:7 * TH],
                         p_sb=B4[:, 0:4096],
                         extra_score_wait=[(s_dve, max(prep_done_dve,
                                                       prev_wo_evac[0]))])
            ctxf_done = s_act.n
            wo_ev = {}
            for m in range(8):
                wbk = 6 + (m % 2)
                woff = (m // 2 % 2) * 256 + (m // 4) * 128
                for k in range(8):
                    wl = []
                    if m == 0 and k == 0:
                        wl = [(s_act, ctxf_done), (s_wdma, wo_ld)]
                    elif k == 0 and m >= 2:
                        wl = [(s_dve, wo_ev[m - 2])]
                    b.emit("pe", lambda e, m=m, k=k, wbk=wbk, woff=woff:
                           e.matmul(bank(wbk, 128, woff),
                                    B3[:, (k * 8 + m) * 128:(k * 8 + m + 1) * 128],
                                    B1[:, 2 * TH + k * 128:2 * TH + (k + 1) * 128],
                                    start=(k == 0), stop=(k == 7)),
                           wait=wl,
                           inc=s_pe if k == 7 else None)
                wc = s_pe.n
                wo_ev[m] = b.emit(
                    "dve", lambda e, m=m, qt=qt, wbk=wbk, woff=woff:
                    e.scalar_tensor_tensor(
                        chs(X, m, qt * 128, 128), bank(wbk, 128, woff),
                        bias_col("bo0", m), chs(X, m, qt * 128, 128),
                        ALU.add, ALU.add),
                    wait=[(s_pe, wc)], inc=s_dve)
            prev_wo_evac[0] = s_dve.n

    def stage_l0_ffn(seq, half, xn_ready):
        XSH = B2[:, 4 * TH:8 * TH]
        xf0 = 0
        for c in range(NCH):
            xf0 = b.emit("dve", lambda e, c=c: e.tensor_copy(
                chs(XF, c), chs(XN, c)),
                wait=[(s_dve, xn_ready)] if c == 0 else (), inc=s_dve)

        for ly in range(2):
            src = XN
            if ly == 1:
                for c in range(NCH):
                    b.emit("dve", lambda e, c=c: e.tensor_copy(
                        chs(XN, c), chs(XF, c)), inc=s_dve)
            for c in range(4):
                if half == 0:
                    b.emit("dve", lambda e, c=c: e.memset(
                        XSH[:, c * TH:c * TH + 1], 0.0), inc=s_dve)
                else:
                    b.emit("dve", lambda e, c=c, ly=ly: e.tensor_copy(
                        XSH[:, c * TH:c * TH + 1],
                        SHC[:, ly * 4 + c:ly * 4 + c + 1]), inc=s_dve)
                b.emit("dve", lambda e, c=c: e.tensor_copy(
                    XSH[:, c * TH + 1:(c + 1) * TH],
                    chs(src, c + 4)[:, 0:TH - 1]), inc=s_dve)
            if half == 0:
                for c in range(4):
                    b.emit("dve", lambda e, c=c, ly=ly: e.tensor_copy(
                        SHC[:, ly * 4 + c:ly * 4 + c + 1],
                        chs(src, c + 4)[:, TH - 1:TH]), inc=s_dve)
            shifts_done = s_dve.n
            rhs_c = lambda k: (chs(src, k) if k < 4 else
                               XSH[:, (k - 4) * TH:(k - 4 + 1) * TH])

            tb_box = [load_tables("t_ffn", 0, 1, half)]
            qfin_gate = [0]

            def qk_evac(m, ps, pe_cnt, ly=ly):
                isq = m < 4
                hp = m if isq else m - 4
                if m == 4:
                    tb_box[0] = load_tables("t_ffn", 2, 3, half)
                a = b.emit("act", lambda e, ps=ps: e.activation(
                    B1[:, 0:TH], ps, AF.Identity),
                    wait=[(s_pe, pe_cnt), (s_xdma, tb_box[0]),
                          (s_dve, s_dve.n)],
                    inc=s_act)
                sqd = b.emit("act", lambda e: e.activation(
                    B1[:, 4 * TH:5 * TH], B1[:, 0:TH], AF.Square), inc=s_act)
                for st in range(2):
                    b.emit("pe", lambda e, st=st: e.matmul(
                        bank(6 + st), BDG[:, :],
                        B1[:, 4 * TH + st * 512:4 * TH + (st + 1) * 512],
                        start=True, stop=True),
                        wait=[(s_act, sqd), (s_dve, qfin_gate[0])]
                        if st == 0 else (),
                        inc=s_pe if st == 1 else None)
                ssd = s_pe.n
                b.emit("act", lambda e: e.activation(
                    MU[:, :], psym(6 * 512, 1024), AF.Ln),
                    wait=[(s_pe, ssd)], inc=s_act)
                ivd = b.emit("act", lambda e: e.activation(
                    MU[:, :], MU[:, :], AF.Exp, scale=-0.5), inc=s_act)
                sc = bias_col("qs%d" % ly) if isq else bias_col("ks%d" % ly)
                if isq:
                    b.emit("dve", lambda e, sc=sc: e.tensor_scalar(
                        B1[:, TH:2 * TH], B1[:, 0:TH], sc, 8.0,
                        ALU.mult, ALU.mult), wait=[(s_act, a)], inc=s_dve)
                else:
                    b.emit("dve", lambda e, sc=sc: e.tensor_scalar_mul(
                        B1[:, TH:2 * TH], B1[:, 0:TH], sc),
                        wait=[(s_act, a)], inc=s_dve)
                _, rd = rope(B1[:, TH:2 * TH], B1[:, 2 * TH:3 * TH], PRH,
                             TBL[:, 0:TH], TBL[:, TH:2 * TH], 4,
                             B1[:, 3 * TH:4 * TH], [(s_dve, s_dve.n)])
                dst = (chs(B2, hp) if isq else chs(B3, hp))
                qfin_gate[0] = b.emit(
                    "dve", lambda e, dst=dst: e.tensor_mul(
                        dst, B1[:, 2 * TH:3 * TH], MU[:, :]),
                    wait=[(s_act, ivd)], inc=s_dve)
                return (s_act, ivd)

            proj_fmaj("f_wqkv", ly, DM, DM, rhs_c, qk_evac,
                      rhs_wait=lambda k: [(s_dve, shifts_done)] if k == 0 else [])
            vld = wload(B4[:, 4096:8192], "f_wv", 0, 8, 0, 512, lay=ly,
                        wait=[(s_pe, s_pe.n)])
            for s in range(8):
                for k in range(8):
                    b.emit("pe", lambda e, s=s, k=k: e.matmul(
                        bank(2, 512), rhs_c(k)[:, s * 128:(s + 1) * 128],
                        B4[:, 4096 + k * 512:4096 + (k + 1) * 512],
                        start=(k == 0), stop=(k == 7)),
                        wait=([(s_wdma, vld), (s_act, s_act.n),
                               (s_dve, s_dve.n)] if (s == 0 and k == 0) else
                              ([(s_dve, s_dve.n)] if k == 0 else ())),
                        inc=s_pe if k == 7 else None)
                vp = s_pe.n
                b.emit("dve", lambda e, s=s: e.tensor_copy(
                    B4[:, s * 512:(s + 1) * 512], bank(2, 512)),
                    wait=[(s_pe, vp)], inc=s_dve)
            vt_done = s_dve.n
            if half == 0:
                for c in range(4):
                    b.emit("dve", lambda e, c=c, ly=ly: e.tensor_copy(
                        KT[:, ly * 2048 + c * 512:ly * 2048 + (c + 1) * 512],
                        chs(B3, c)[:, 512:1024]), inc=s_dve)
                b.emit("dve", lambda e, ly=ly: e.tensor_copy(
                    VTT[:, ly * 2048:(ly + 1) * 2048], B4[:, 2048:4096]),
                    inc=s_dve)
            prep = s_dve.n

            def k_ap(hp, stile, h64, ly=ly):
                if stile >= half * 8:
                    sl = stile - half * 8
                    return chs(B3, hp)[h64 * 64:(h64 + 1) * 64,
                                       sl * 128:(sl + 1) * 128]
                base = ly * 2048 + hp * 512
                off = (stile - 4) * 128
                return KT[h64 * 64:(h64 + 1) * 64, base + off:base + off + 128]

            def v_ap(hp, stile, h64, ly=ly):
                h = 2 * hp + h64
                if stile >= half * 8:
                    sl = stile - half * 8
                    return B4[:, sl * 512 + h * 64:sl * 512 + (h + 1) * 64]
                base = ly * 2048 + (stile - 4) * 512
                return VTT[:, base + h * 64:base + (h + 1) * 64]

            prev_wo_evac = [s_dve.n]
            prev_ctxf_evac = [s_act.n]
            wo_ld = b.dma("pool", B3[:, 4096:8192], dr["f_wo"][ly, :, :],
                          s_wdma, wait=[(s_pe, s_pe.n)])
            for qt in range(8):
                qg = half * 8 + qt
                s_lo = max(0, qg - 4)
                tiles = [(s, ('causal' if s == qg else
                              ('band' if (s == qg - 4) else None)))
                         for s in range(s_lo, qg + 1)]

                def q_ap(hp, h64, qt=qt):
                    return B2[h64 * 64:(h64 + 1) * 64,
                              hp * TH + qt * 128:hp * TH + (qt + 1) * 128]

                def ctx_out(hp, qt=qt):
                    dst = B1[:, TH + (hp % 2) * 128:TH + (hp % 2 + 1) * 128]
                    def post(hp=hp, dst=dst):
                        td = s_dve.n
                        sl = 3 * 512 + (hp % 2) * 64
                        wl = [(s_dve, td)]
                        if hp >= 2:
                            wl.append((s_act, prev_ctxf_evac[0]))
                        b.emit("pe", lambda e, dst=dst, sl=sl: e.transpose(
                            psym(sl, 64).bitcast(BF16), dst, IDB[:, :]),
                            wait=wl, inc=s_pe)
                        tc = s_pe.n
                        prev_ctxf_evac[0] = b.emit(
                            "act", lambda e, hp=hp, sl=sl: e.copy(
                                B1[:, 2 * TH + hp * 128:2 * TH + (hp + 1) * 128],
                                psym(sl, 64).bitcast(BF16)),
                            wait=[(s_pe, tc)], inc=s_act)
                    return dst, post

                attn_softmax(qt, list(range(4)), q_ap, k_ap, tiles, v_ap,
                             ctx_out, pt_sb=B1[:, 3 * TH:7 * TH],
                             p_sb=B4[:, 4096:8192],
                             extra_score_wait=[(s_dve, max(prep,
                                                           prev_wo_evac[0]))])
                ctxf_done = s_act.n
                wo_ev = {}
                for m in range(8):
                    wbk = 6 + (m % 2)
                    woff = (m // 2 % 2) * 256 + (m // 4) * 128
                    for k in range(4):
                        wl = []
                        if m == 0 and k == 0:
                            wl = [(s_act, ctxf_done), (s_wdma, wo_ld)]
                        elif k == 0 and m >= 2:
                            wl = [(s_dve, wo_ev[m - 2])]
                        b.emit("pe", lambda e, m=m, k=k, wbk=wbk, woff=woff:
                               e.matmul(bank(wbk, 128, woff),
                                        B3[:, 4096 + (k * 8 + m) * 128:
                                           4096 + (k * 8 + m + 1) * 128],
                                        B1[:, 2 * TH + k * 128:
                                           2 * TH + (k + 1) * 128],
                                        start=(k == 0), stop=(k == 3)),
                               wait=wl, inc=s_pe if k == 3 else None)
                    wc = s_pe.n
                    wo_ev[m] = b.emit(
                        "dve", lambda e, m=m, qt=qt, wbk=wbk, woff=woff:
                        e.tensor_add(chs(XF, m, qt * 128, 128),
                                     bank(wbk, 128, woff),
                                     chs(XF, m, qt * 128, 128)),
                        wait=[(s_pe, wc)], inc=s_dve)
                prev_wo_evac[0] = s_dve.n

            for c in range(NCH):
                b.emit("dve", lambda e, c=c: e.tensor_copy(
                    chs(XN, c), chs(XF, c)), inc=s_dve)
            for c in range(4):
                if half == 0:
                    b.emit("dve", lambda e, c=c: e.memset(
                        XSH[:, c * TH:c * TH + 1], 0.0), inc=s_dve)
                else:
                    b.emit("dve", lambda e, c=c, ly=ly: e.tensor_copy(
                        XSH[:, c * TH:c * TH + 1],
                        SHC[:, 8 + ly * 4 + c:8 + ly * 4 + c + 1]), inc=s_dve)
                b.emit("dve", lambda e, c=c: e.tensor_copy(
                    XSH[:, c * TH + 1:(c + 1) * TH],
                    chs(XN, c + 4)[:, 0:TH - 1]), inc=s_dve)
            if half == 0:
                for c in range(4):
                    b.emit("dve", lambda e, c=c, ly=ly: e.tensor_copy(
                        SHC[:, 8 + ly * 4 + c:8 + ly * 4 + c + 1],
                        chs(XN, c + 4)[:, TH - 1:TH]), inc=s_dve)
            cast2 = s_dve.n
            ln2src = lambda c: (chs(XN, c) if c < 4 else
                                XSH[:, (c - 4) * TH:(c - 4 + 1) * TH])
            _, act_st = ln_stats(ln2src, NCH, 1.0 / DM,
                                 src_wait=lambda c: [(s_dve, cast2)])
            gname, bname = "g%d" % ly, "lb%d" % ly
            for c in range(NCH):
                b.emit("dve", lambda e, c=c: e.tensor_sub(
                    chs(B2, c), ln2src(c), MU[:, :]),
                    wait=[(s_act, act_st)] if c == 0 else (), inc=s_dve)
                b.emit("dve", lambda e, c=c: e.tensor_mul(
                    chs(B2, c), chs(B2, c), IV[:, :]), inc=s_dve)
                b.emit("dve", lambda e, c=c, gname=gname, bname=bname:
                       e.tensor_scalar(chs(B2, c), chs(B2, c),
                                       bias_col(gname, c), bias_col(bname, c),
                                       ALU.mult, ALU.add), inc=s_dve)
            hn_done = s_dve.n

            ngrp = (22 + 7) // 8
            w2_out_gate = [(s_dve, s_dve.n)]
            for g in range(ngrp):
                jlist = list(range(g * 8, min(22, (g + 1) * 8)))
                hg_done = [0]
                ev_prev = {}
                for ji, j in enumerate(jlist):
                    sl = ji % 3
                    bs = (ji % 2) * 2
                    gld = wload(WA[:, sl * TH:sl * TH + 1024], "f_w1", 0, 8,
                                2816 + j * 128, 128, lay=ly,
                                wait=[(s_pe, s_pe.n)] if ji < 3 else
                                [(s_pe, ev_prev[jlist[ji - 3]][1])])
                    for k in range(NCH):
                        for st in range(2):
                            wl = []
                            if k == 0 and st == 0:
                                wl.append((s_wdma, gld))
                                if ji == 0 and g == 0:
                                    wl.append((s_dve, hn_done))
                                if ji - 2 >= 0 and jlist[ji - 2] in ev_prev:
                                    wl.append(ev_prev[jlist[ji - 2]][0])
                            b.emit("pe", lambda e, k=k, st=st, bs=bs, sl=sl:
                                   e.matmul(bank(bs + st),
                                            WA[:, sl * TH + k * 128:
                                               sl * TH + (k + 1) * 128],
                                            chs(B2, k)[:, st * 512:
                                                       (st + 1) * 512],
                                            start=(k == 0), stop=(k == 7)),
                                   wait=wl,
                                   inc=s_pe if (k == 7 and st == 1) else None)
                    gmm = s_pe.n
                    ga = b.emit("act", lambda e, bs=bs, j=j, ly=ly:
                                e.activation(B1[:, 0:TH], bank(bs, 1024),
                                             AF.Gelu,
                                             bias=bias_col("b1_%d" % ly,
                                                           22 + j)),
                                wait=[(s_pe, gmm)], inc=s_act)
                    bs2 = 4 + (ji % 2) * 2
                    ald = wload(WA[:, sl * TH + 1024:sl * TH + 1024 + 1024]
                                if False else WB[:, (ji % 2) * 1024:
                                                 (ji % 2) * 1024 + 1024],
                                "f_w1", 0, 8, j * 128, 128, lay=ly,
                                wait=[(s_pe, s_pe.n)])
                    for k in range(NCH):
                        for st in range(2):
                            wl = []
                            if k == 0 and st == 0:
                                wl.append((s_wdma, ald))
                                if ji - 2 >= 0 and jlist[ji - 2] in ev_prev:
                                    wl.append(ev_prev[jlist[ji - 2]][2])
                            b.emit("pe", lambda e, k=k, st=st, bs2=bs2, ji=ji:
                                   e.matmul(bank(bs2 + st),
                                            WB[:, (ji % 2) * 1024 + k * 128:
                                               (ji % 2) * 1024 + (k + 1) * 128],
                                            chs(B2, k)[:, st * 512:
                                                       (st + 1) * 512],
                                            start=(k == 0), stop=(k == 7)),
                                   wait=wl,
                                   inc=s_pe if (k == 7 and st == 1) else None)
                    amm = s_pe.n
                    hg_done[0] = b.emit(
                        "dve", lambda e, bs2=bs2, j=j, ji=ji, ly=ly:
                        e.scalar_tensor_tensor(
                            B3[:, ji * TH:(ji + 1) * TH], bank(bs2, 1024),
                            bias_col("b1_%d" % ly, j), B1[:, 0:TH],
                            ALU.add, ALU.mult),
                        wait=[(s_pe, amm), (s_act, ga)], inc=s_dve)
                    ev_prev[j] = ((s_act, ga), gmm, (s_dve, hg_done[0]))
                hg_all = s_dve.n
                for mh in range(2):
                    for st in range(2):
                        w2e_gate = w2_out_gate[0]
                        jcnt = {}
                        for ji, j in enumerate(jlist):
                            w2ld = wload(WA[:, (ji % 3) * TH:
                                             (ji % 3) * TH + 1024],
                                         "f_w2", (g * 8 + ji) * 128,
                                         1, 0, 1024, lay=ly,
                                         wait=[(s_pe, jcnt[ji - 3])]
                                         if ji >= 3 else [(s_pe, s_pe.n)])
                            for mi in range(4):
                                m = mh * 4 + mi
                                wl = []
                                if mi == 0:
                                    wl.append((s_wdma, w2ld))
                                    wl.append((s_dve, hg_all))
                                    if ji == 0:
                                        wl.append(w2e_gate)
                                b.emit("pe", lambda e, ji=ji, mi=mi, m=m,
                                       st=st, j=j, nj=len(jlist):
                                       e.matmul(bank(4 + mi),
                                                WA[:, (ji % 3) * TH + m * 128:
                                                   (ji % 3) * TH + (m + 1) * 128],
                                                B3[:, ji * TH + st * 512:
                                                   ji * TH + st * 512 + 512],
                                                start=(ji == 0),
                                                stop=(ji == nj - 1)),
                                       wait=wl,
                                       inc=s_pe if mi == 3 else None)
                            jcnt[ji] = s_pe.n
                        wmm = s_pe.n
                        for mi in range(4):
                            m = mh * 4 + mi
                            if g < ngrp - 1:
                                b.emit("dve", lambda e, m=m, st=st, mi=mi:
                                       e.tensor_add(
                                           chs(XF, m, st * 512, 512),
                                           bank(4 + mi),
                                           chs(XF, m, st * 512, 512)),
                                       wait=[(s_pe, wmm)] if mi == 0 else (),
                                       inc=s_dve)
                            else:
                                b.emit("dve", lambda e, m=m, st=st, mi=mi,
                                       ly=ly: e.scalar_tensor_tensor(
                                           chs(XF, m, st * 512, 512),
                                           bank(4 + mi),
                                           bias_col("b2_%d" % ly, m),
                                           chs(XF, m, st * 512, 512),
                                           ALU.add, ALU.add),
                                       wait=[(s_pe, wmm)] if mi == 0 else (),
                                       inc=s_dve)
                        w2_out_gate[0] = (s_dve, s_dve.n)
        for c in range(NCH):
            b.emit("dve", lambda e, c=c: e.tensor_add(
                chs(X, c), chs(X, c), chs(XF, c)), inc=s_dve)

    def stage_l1(seq, half, xn_ready):
        bvt_ld = b.dma("sp", B3[:, 0:2112].bitcast(F32), dr["l1_bvt"][:, :],
                       s_xdma, wait=[(s_dve, s_dve.n), (s_pe, s_pe.n),
                                     (s_act, s_act.n)])
        BVT1 = B3[:, 0:2112].bitcast(F32)
        wv_ld = wload(B2[:, :], "l1_wv", 0, 8, 0, 1024,
                      wait=[(s_pe, s_pe.n)])
        wn_ld = wload(WB[:, 0:256], "l1_wn", 0, 8, 0, 32,
                      wait=[(s_pe, s_pe.n)])
        ev_g = [0]
        for s in range(8):
            for k in range(8):
                for st in range(2):
                    b.emit("pe", lambda e, s=s, k=k, st=st: e.matmul(
                        bank(st), chs(XN, k)[:, s * 128:(s + 1) * 128],
                        B2[:, k * 1024 + st * 512:k * 1024 + (st + 1) * 512],
                        start=(k == 0), stop=(k == 7)),
                        wait=([(s_wdma, wv_ld), (s_dve, xn_ready)]
                              if (s == 0 and k == 0 and st == 0) else
                              ([(s_dve, ev_g[0])] if k == 0 and st == 0
                               else ())),
                        inc=None)
                b.emit("pe", lambda e, s=s, k=k: e.matmul(
                    bank(2, 32), chs(XN, k)[:, s * 128:(s + 1) * 128],
                    WB[:, k * 32:(k + 1) * 32],
                    start=(k == 0), stop=(k == 7)),
                    wait=[(s_wdma, wn_ld)] if (s == 0 and k == 0) else (),
                    inc=s_pe if k == 7 else None)
            vp = s_pe.n
            for st in range(2):
                b.emit("dve", lambda e, s=s, st=st: e.tensor_add(
                    B4[:, s * 1024 + st * 512:s * 1024 + (st + 1) * 512],
                    bank(st), BVT1[:, st * 512:(st + 1) * 512]),
                    wait=[(s_pe, vp), (s_xdma, bvt_ld)]
                    if st == 0 else (), inc=s_dve)
            b.emit("dve", lambda e, s=s: e.tensor_add(
                NTT[:, s * 16:(s + 1) * 16], bank(2, 16),
                BVT1[:, 1024:1040]), inc=s_dve)
            b.emit("dve", lambda e, s=s: e.tensor_copy(
                LRT[:, s * 8:(s + 1) * 8], bank(2, 8, 16)), inc=s_dve)
            ev_g[0] = s_dve.n
        vproj_done = s_dve.n

        lrt3 = LRT[:, :].rearrange("p (s e) -> p s e", e=8)
        b.emit("dve", lambda e: e.reduce_max(RST[:, 0:8], lrt3, axis=AX.X),
               wait=[(s_dve, vproj_done)], inc=s_dve)
        rmx = s_dve.n
        for s in range(8):
            b.emit("dve", lambda e, s=s: e.tensor_single_scalar(
                LRT[:, s * 8:(s + 1) * 8], LRT[:, s * 8:(s + 1) * 8],
                RST[:, s:s + 1], ALU.subtract),
                wait=[(s_dve, rmx)] if s == 0 else (), inc=s_dve)
        sub_r = s_dve.n
        b.emit("act", lambda e: e.activation(MU[:, 0:64], LRT[:, :], AF.Exp),
               wait=[(s_dve, sub_r)], inc=s_act)
        exp_r = s_act.n
        prb3 = MU[:, 0:64].rearrange("p (s e) -> p s e", e=8)
        b.emit("dve", lambda e: e.reduce_max(RST[:, 8:16], prb3, axis=AX.X),
               wait=[(s_act, exp_r)], inc=s_dve)
        p1d = s_dve.n
        for s in range(8):
            b.emit("dve", lambda e, s=s: e.tensor_single_scalar(
                MU[:, 64 + s * 8:64 + (s + 1) * 8], MU[:, s * 8:(s + 1) * 8],
                RST[:, 8 + s:9 + s], ALU.is_equal),
                wait=[(s_dve, p1d)] if s == 0 else (), inc=s_dve)
        b.emit("dve", lambda e: e.tensor_scalar(
            MU[:, 64:128], MU[:, 64:128], -1e30, None, ALU.mult),
            inc=s_dve)
        b.emit("dve", lambda e: e.tensor_add(MU[:, 64:128], MU[:, 64:128],
                                             MU[:, 0:64]), inc=s_dve)
        sup_d = s_dve.n
        prb4 = MU[:, 64:128].rearrange("p (s e) -> p s e", e=8)
        b.emit("dve", lambda e: e.reduce_max(RST[:, 16:24], prb4, axis=AX.X),
               wait=[(s_dve, sup_d)], inc=s_dve)
        b.emit("dve", lambda e: e.tensor_add(RST[:, 24:32], RST[:, 8:16],
                                             RST[:, 16:24]), inc=s_dve)
        sadd = s_dve.n
        for s in range(8):
            b.emit("dve", lambda e, s=s: e.reciprocal(
                RST[:, 32 + s:33 + s], RST[:, 24 + s:25 + s]),
                wait=[(s_dve, sadd)] if s == 0 else (), inc=s_dve)
        b.emit("dve", lambda e: e.tensor_mul(RST[:, 40:48], RST[:, 24:32],
                                             RST[:, 32:40]), inc=s_dve)
        m_done = s_dve.n
        cp_n = [0] * 8
        for s_ in range(8):
            slot = 2048 + (s_ % 2) * 512
            wl = [(s_dve, m_done)] if s_ == 0 else []
            if s_ >= 2:
                wl.append((s_act, cp_n[s_ - 2]))
            b.emit("pe", lambda e, s_=s_, slot=slot: e.transpose(
                PS[0:1, slot:slot + 128],
                RST[:, 40 + s_:41 + s_], IDF[:, :]),
                wait=wl, inc=s_pe)
            trm = s_pe.n
            cp_n[s_] = b.emit("act", lambda e, s_=s_, slot=slot: e.copy(
                B1[0:1, 2048 + 128 * s_:2176 + 128 * s_],
                PS[0:1, slot:slot + 128]),
                wait=[(s_pe, trm)], inc=s_act)
        mrow_a = s_act.n
        for st in range(2):
            b.emit("pe", lambda e, st=st: e.matmul(
                bank(2 + st), ONE[0:1, 0:128],
                B1[0:1, 2048 + st * 512:2048 + (st + 1) * 512],
                start=True, stop=True),
                wait=[(s_act, mrow_a)] if st == 0 else (),
                inc=s_pe if st == 1 else None)
        msb_mm = s_pe.n
        b.emit("dve", lambda e: e.tensor_copy(IV[:, :], psym(2 * 512, 1024)),
               wait=[(s_pe, msb_mm)], inc=s_dve)
        ms_done = s_dve.n

        ngrp = 4
        w2_out_gate = [(s_dve, s_dve.n)]
        for g in range(ngrp):
            jlist = list(range(g * 8, (g + 1) * 8))
            ev_prev = {}
            for ji, j in enumerate(jlist):
                sl = ji % 3
                bs = (ji % 2) * 2
                gld = wload(WA[:, sl * TH:sl * TH + 1024], "g_w1", 0, 8,
                            j * 128, 128,
                            wait=[(s_pe, s_pe.n)] if ji < 3 or g == 0 else
                            [(s_pe, ev_prev.get(jlist[ji - 3], s_pe.n)
                              if ji >= 3 else s_pe.n)])
                for k in range(NCH):
                    for st in range(2):
                        wl = []
                        if k == 0 and st == 0:
                            wl.append((s_wdma, gld))
                            if ji == 0 and g == 0:
                                wl.append((s_dve, ms_done))
                            if ji - 2 >= 0 and jlist[ji - 2] in ev_prev:
                                wl.append((s_act, ev_prev[jlist[ji - 2]]))
                        b.emit("pe", lambda e, k=k, st=st, bs=bs, sl=sl:
                               e.matmul(bank(bs + st),
                                        WA[:, sl * TH + k * 128:
                                           sl * TH + (k + 1) * 128],
                                        chs(XN, k)[:, st * 512:(st + 1) * 512],
                                        start=(k == 0), stop=(k == 7)),
                               wait=wl,
                               inc=s_pe if (k == 7 and st == 1) else None)
                gmm = s_pe.n
                ev_prev[j] = b.emit("act", lambda e, bs=bs, j=j:
                                    e.activation(B3[:, (j % 8) * TH:
                                                    (j % 8 + 1) * TH],
                                                 bank(bs, 1024),
                                                 AF.Gelu_apprx_tanh,
                                                 bias=bias_col("gb1", j)),
                                    wait=[(s_pe, gmm)], inc=s_act)
            hg_all = s_act.n
            for mh in range(2):
                for st in range(2):
                    w2e_gate = w2_out_gate[0]
                    jcnt = {}
                    for ji, j in enumerate(jlist):
                        w2ld = wload(WA[:, (ji % 3) * TH:(ji % 3) * TH + 1024],
                                     "g_w2", j * 128, 1, 0, 1024,
                                     wait=[(s_pe, jcnt[ji - 3])]
                                     if ji >= 3 else [(s_pe, s_pe.n)])
                        for mi in range(4):
                            m = mh * 4 + mi
                            wl = []
                            if mi == 0:
                                wl.append((s_wdma, w2ld))
                                wl.append((s_act, hg_all))
                                if ji == 0:
                                    wl.append(w2e_gate)
                            b.emit("pe", lambda e, ji=ji, mi=mi, m=m, st=st:
                                   e.matmul(bank(4 + mi),
                                            WA[:, (ji % 3) * TH + m * 128:
                                               (ji % 3) * TH + (m + 1) * 128],
                                            B3[:, ji * TH + st * 512:
                                               ji * TH + st * 512 + 512],
                                            start=(ji == 0), stop=(ji == 7)),
                                   wait=wl, inc=s_pe if mi == 3 else None)
                        jcnt[ji] = s_pe.n
                    wmm = s_pe.n
                    for mi in range(4):
                        m = mh * 4 + mi
                        if g == 0:
                            b.emit("dve", lambda e, m=m, st=st, mi=mi:
                                   e.tensor_copy(chs(XF, m, st * 512, 512),
                                                 bank(4 + mi)),
                                   wait=[(s_pe, wmm)] if mi == 0 else (),
                                   inc=s_dve)
                        elif g < ngrp - 1:
                            b.emit("dve", lambda e, m=m, st=st, mi=mi:
                                   e.tensor_add(chs(XF, m, st * 512, 512),
                                                bank(4 + mi),
                                                chs(XF, m, st * 512, 512)),
                                   wait=[(s_pe, wmm)] if mi == 0 else (),
                                   inc=s_dve)
                        else:
                            b.emit("dve", lambda e, m=m, st=st, mi=mi:
                                   e.scalar_tensor_tensor(
                                       chs(XF, m, st * 512, 512),
                                       bank(4 + mi), bias_col("gb2", m),
                                       chs(XF, m, st * 512, 512),
                                       ALU.add, ALU.add),
                                   wait=[(s_pe, wmm)] if mi == 0 else (),
                                   inc=s_dve)
                    w2_out_gate[0] = (s_dve, s_dve.n)
        for c in range(NCH):
            b.emit("dve", lambda e, c=c: e.tensor_mul(
                chs(XF, c), chs(XF, c), IV[:, :]), inc=s_dve)
            b.emit("dve", lambda e, c=c: e.tensor_add(
                chs(X, c), chs(X, c), chs(XF, c)), inc=s_dve)
        ffn_done = s_dve.n

        if half == 0:
            b.emit("dve", lambda e: e.memset(MSB[:, :], 0.0), inc=s_dve)
            b.emit("dve", lambda e: e.memset(MBF[:, :], 0.0), inc=s_dve)
        def qk1_evac(m, ps, pe_cnt):
            dst = chs(B2, m) if m < 8 else chs(B3, m - 8)
            a = b.emit("act", lambda e, m=m, ps=ps, dst=dst: e.activation(
                dst, ps, AF.Identity, bias=bias_col("bqk", m)),
                wait=[(s_pe, pe_cnt)], inc=s_act)
            return (s_act, a)
        proj_fmaj("l1_wqk", None, DM, 2048, lambda k: chs(XN, k), qk1_evac,
                  rhs_wait=lambda k: [(s_dve, ffn_done)] if k == 0 else [])
        qk_done = s_act.n

        prev_invnb = [s_act.n]
        ctxev = [0, 0]
        xncp = [0, 0]
        kttp = [0, 0]
        msbp = [0, 0]
        last_ct = [0]
        nrwp = [0]
        mbf_dn = [0]
        for c in range(8):
            b.emit("pe", lambda e, c=c: e.transpose(
                PS[0:16, 4 * 512:4 * 512 + 128],
                NTT[:, c * 16:(c + 1) * 16], IDF[:, :]),
                wait=[(s_act, prev_invnb[0])] +
                     ([(s_dve, nrwp[0])] if nrwp[0] else []), inc=s_pe)
            trn = s_pe.n
            b.emit("dve", lambda e: e.tensor_scalar_mul(
                NRW[0:16, 0:128], psym(4 * 512, 128)[0:16, :], -1.0),
                wait=[(s_pe, trn)], inc=s_dve)
            nrwp[0] = s_dve.n
            for st in range(2):
                b.emit("pe", lambda e, st=st: e.matmul(
                    bank(2 + st), NRW[0:16, 0:128],
                    BLK16[0:16, st * 512:(st + 1) * 512],
                    start=True, stop=True),
                    wait=[(s_dve, nrwp[0])] if st == 0 else (),
                    inc=s_pe if st == 1 else None)
            nb_mm = s_pe.n
            prev_invnb[0] = b.emit("act", lambda e: e.activation(
                B1[:, 0:TH], psym(2 * 512, 1024), AF.Exp),
                wait=[(s_pe, nb_mm)] +
                     ([(s_dve, last_ct[0])] if last_ct[0] else []), inc=s_act)
            invnb_d = prev_invnb[0]

            for hp in range(8):
                par = hp % 2
                for h64 in range(2):
                    tp = (0, 0) if h64 == 0 else (64, 0)
                    if hp == 0 and h64 == 0 and c == 0:
                        qw = [(s_act, qk_done), (s_dve, s_dve.n)]
                    elif h64 == 0:
                        qw = []
                        gate = mbf_dn[0] if hp == 0 else ctxev[par]
                        if gate:
                            qw.append((s_dve, gate))
                    else:
                        qw = []
                    b.emit("pe", lambda e, hp=hp, c=c, h64=h64, tp=tp,
                           par=par: e.matmul(
                               bank(1, 64, par * 128 + h64 * 64),
                               B2[h64 * 64:(h64 + 1) * 64,
                                  hp * TH + c * 128:hp * TH + (c + 1) * 128],
                               MBF[h64 * 64:(h64 + 1) * 64,
                                   hp * 64:(hp + 1) * 64],
                               start=True, stop=False, tile_position=tp),
                           wait=qw, inc=None)
                    b.emit("pe", lambda e, hp=hp, c=c, h64=h64, tp=tp,
                           par=par: e.matmul(
                               bank(0, 128, (par * 2 + h64) * 128),
                               B3[h64 * 64:(h64 + 1) * 64,
                                  hp * TH + c * 128:hp * TH + (c + 1) * 128],
                               B2[h64 * 64:(h64 + 1) * 64,
                                  hp * TH + c * 128:hp * TH + (c + 1) * 128],
                               start=True, stop=True, tile_position=tp),
                           inc=s_pe)
                    smm = s_pe.n
                    b.emit("dve", lambda e, h64=h64, par=par: e.tensor_mul(
                        B1[:, TH + (par * 2 + h64) * 128:
                           TH + (par * 2 + h64 + 1) * 128],
                        bank(0, 128, (par * 2 + h64) * 128), MT[:, :]),
                        wait=[(s_pe, smm)], inc=s_dve)
                    ptd = s_dve.n
                    b.emit("pe", lambda e, hp=hp, c=c, h64=h64, par=par:
                           e.matmul(bank(1, 64, par * 128 + h64 * 64),
                                    B1[:, TH + (par * 2 + h64) * 128:
                                       TH + (par * 2 + h64 + 1) * 128],
                                    B4[:, c * 1024 + (2 * hp + h64) * 64:
                                       c * 1024 + (2 * hp + h64 + 1) * 64],
                                    start=False, stop=True),
                           wait=[(s_dve, ptd)],
                           inc=s_pe if h64 == 1 else None)
                ctxmm = s_pe.n
                ct = b.emit("dve", lambda e, hp=hp, par=par: e.tensor_mul(
                    B1[:, TH + 512 + par * 128:TH + 512 + (par + 1) * 128],
                    bank(1, 128, par * 128),
                    B1[:, hp * 128:(hp + 1) * 128]),
                    wait=[(s_pe, ctxmm), (s_act, invnb_d)], inc=s_dve)
                ctxev[par] = ct
                last_ct[0] = ct
                wl = [(s_dve, ct)]
                if xncp[par]:
                    wl.append((s_act, xncp[par]))
                b.emit("pe", lambda e, par=par: e.transpose(
                    psym(5 * 512 + par * 64, 64).bitcast(BF16),
                    B1[:, TH + 512 + par * 128:TH + 512 + (par + 1) * 128],
                    IDB[:, :]),
                    wait=wl, inc=s_pe)
                tc = s_pe.n
                xncp[par] = b.emit(
                    "act", lambda e, hp=hp, c=c, par=par: e.copy(
                        XN[:, hp * TH + c * 128:hp * TH + (c + 1) * 128],
                        psym(5 * 512 + par * 64, 64).bitcast(BF16)),
                    wait=[(s_pe, tc)], inc=s_act)
                b.emit("pe", lambda e, hp=hp, c=c, par=par: e.transpose(
                    psym(6 * 512 + par * 64, 64).bitcast(BF16),
                    B3[:, hp * TH + c * 128:hp * TH + (c + 1) * 128],
                    IDB[:, :]),
                    wait=[(s_act, kttp[par])] if kttp[par] else (),
                    inc=s_pe)
                tk = s_pe.n
                ktt = b.emit("act", lambda e, par=par: e.copy(
                    B1[:, TH + 768 + par * 128:TH + 896 + par * 128],
                    psym(6 * 512 + par * 64, 64).bitcast(BF16)),
                    wait=[(s_pe, tk)], inc=s_act)
                kttp[par] = ktt
                for h64 in range(2):
                    wl = []
                    if h64 == 0:
                        wl.append((s_act, ktt))
                        if msbp[par]:
                            wl.append((s_dve, msbp[par]))
                    b.emit("pe", lambda e, hp=hp, c=c, h64=h64, par=par:
                           e.matmul(bank(7, 64, par * 128 + h64 * 64),
                                    B1[:, TH + 768 + par * 128:
                                       TH + 896 + par * 128],
                                    B4[:, c * 1024 + (2 * hp + h64) * 64:
                                       c * 1024 + (2 * hp + h64 + 1) * 64],
                                    start=True, stop=True),
                           wait=wl, inc=s_pe if h64 == 1 else None)
                dmm = s_pe.n
                b.emit("dve", lambda e, hp=hp, par=par: e.tensor_add(
                    MSB[0:64, hp * 64:(hp + 1) * 64],
                    MSB[0:64, hp * 64:(hp + 1) * 64],
                    bank(7, 64, par * 128)[0:64, :]),
                    wait=[(s_pe, dmm)], inc=s_dve)
                b.emit("dve", lambda e, hp=hp, par=par: e.tensor_add(
                    MSB[64:128, hp * 64:(hp + 1) * 64],
                    MSB[64:128, hp * 64:(hp + 1) * 64],
                    bank(7, 64, par * 128 + 64)[64:128, :]),
                    inc=s_dve)
                msbp[par] = s_dve.n
            b.emit("dve", lambda e: e.tensor_copy(MBF[:, :], MSB[:, :]),
                   inc=s_dve)
            mbf_dn[0] = s_dve.n

        ctx_ready = s_act.n

        def wo1_evac(m, ps, pe_cnt):
            d = b.emit("dve", lambda e, m=m, ps=ps: e.scalar_tensor_tensor(
                chs(X, m), ps, bias_col("bo1", m), chs(X, m),
                ALU.add, ALU.add), wait=[(s_pe, pe_cnt)], inc=s_dve)
            return (s_dve, d)
        proj_fmaj("l1_wo", None, DM, DM, lambda k: chs(XN, k), wo1_evac,
                  rhs_wait=lambda k: [(s_act, ctx_ready)] if k == 0 else [])

    for nm in dump_bufs:
        b.emit("pool", lambda e, buf=sb[nm]: e.memset(buf[:, :], 0.0))
    for dst, src in ((BIA, "bias"), (BVT0, "l0_bvt"), (MCS, "m_causal"),
                     (MBS, "m_band"), (MT, "m_tril"), (IDB, "c_id_bf"),
                     (IDF, "c_id_f32"), (ONE, "c_ones_bf"),
                     (BDG, "c_bdiag_bf"), (PRP, "c_permpair"),
                     (PRH, "c_permhalf"), (BLK16, "c_blk16")):
        b.dma("sp", dst[:, :], dr[src][:, :], s_xdma)
    bar()

    x_ready = 0
    for seq, half in units:
        x_ready = load_x(seq, half)
        bar()
        xn_done = ln_top(x_ready)
        if "l0a" in stages:
            stage_l0_attn(seq, half, xn_done)
            bar()
        if "l0f" in stages:
            stage_l0_ffn(seq, half, xn_done)
        bar()
        if "d_mid" in dbg_t and (seq, half) == (0, 1):
            b.dma("sp", dbg_t["d_mid"][:, :], X[:, :], s_out,
                  wait=[(s_dve, s_dve.n)])
            bar()
        if "l1" in stages:
            xn2_done = ln_top(x_ready)
            stage_l1(seq, half, xn2_done)
            bar()
        if "d_xn" in dbg_t:
            for c in range(NCH):
                b.emit("dve", lambda e, c=c: e.tensor_copy(chs(XF, c), chs(XN, c)),
                       inc=s_dve)
            b.dma("sp", dbg_t["d_xn"][:, :], XF[:, :], s_out,
                  wait=[(s_dve, s_dve.n)])
        if "d_x" in dbg_t:
            b.dma("sp", dbg_t["d_x"][:, :], X[:, :], s_out,
                  wait=[(s_dve, s_dve.n)])
        bar()
        store_x(seq, half, wait=[(s_dve, s_dve.n)])
        bar()

    if dump_bufs:
        s_dmp = b.sem("dmp")
        s_dmp.h = stack.enter_context(nc.semaphore("dmp"))
        for nm in dump_bufs:
            buf = sb[nm]
            dt_ = nc.dram_tensor("dump_" + nm, [128, buf.shape[1]], F32,
                                 kind="ExternalOutput").ap()
            b.dma("pool", dt_[:, :], buf[:, :], s_dmp,
                  wait=[(s_dve, s_dve.n), (s_act, s_act.n), (s_pe, s_pe.n)])
        b.emit("pool", lambda e: e.wait_ge(s_dmp.h, s_dmp.n))

    b.emit("sp", lambda e: e.wait_ge(s_out.h, s_out.n))
    b.replay()
    return nc, stack, dbg_t


def kernel(**inputs):
    H = _host_prep(inputs)
    nc, stack, _ = build_program()
    in_maps = []
    for c in range(8):
        m = dict(H)
        m["xin"] = _per_core_x(inputs, c)
        in_maps.append(m)
    res = run_bass_kernel_spmd(nc, in_maps, core_ids=list(range(8)))
    stack.close()
    out = np.empty((16, TT, DM), np.float32)
    for c in range(8):
        y = res.results[c]["yout"]
        out[2 * c] = y[0].T
        out[2 * c + 1] = y[1].T
    return out
